# version 02
# baseline (speedup 1.0000x reference)
import numpy as np
import ml_dtypes
import concourse.bass as bass
import concourse.mybir as mybir
from concourse.bass_utils import run_bass_kernel_spmd
from contextlib import ExitStack

F32 = mybir.dt.float32
BF16 = mybir.dt.bfloat16
AF = mybir.ActivationFunctionType
OP = mybir.AluOpType
AX = mybir.AxisListType.X
BF = ml_dtypes.bfloat16

NCORE = 8
B, T, D = 2, 8192, 1024
TPC = (B * T) // NCORE


class Prog:

    def __init__(self):
        self.items = []

    def t(self, fn):
        self.items.append(("tensor", fn, 1))

    def v(self, fn):
        self.items.append(("vector", fn, 1))

    def s(self, fn):
        self.items.append(("scalar", fn, 1))

    def dma(self, out, in_):
        self.items.append(("sync", lambda e, o=out, i=in_: e.dma_start(out=o, in_=i), 16))

    def emit(self, nc, S, block):
        items = self.items
        cum = [0]
        for (_, _, inc) in items:
            cum.append(cum[-1] + inc)
        total = cum[-1]
        engines = ["sync", "tensor", "vector", "scalar"]

        def mk(name):
            def f(eng):
                for i, (en, fn, inc) in enumerate(items):
                    if en != name:
                        continue
                    if cum[i] > 0:
                        eng.wait_ge(S, cum[i])
                    fn(eng).then_inc(S, inc)
                if name == "sync":
                    eng.wait_ge(S, total)
            return f

        for en in engines:
            getattr(block, en)(mk(en))


def ln_plain(p, buf, W, sumc, negm, stdc, sqscr, accc, epsc):
    nch = (W + 511) // 512
    p.v(lambda e, b=buf: e.reduce_sum(out=sumc[:], in_=b, axis=AX))
    p.v(lambda e: e.tensor_scalar_mul(out=negm[:], in0=sumc[:], scalar1=-1.0 / W))
    p.s(lambda e, b=buf: e.activation(out=b, in_=b, func=AF.Identity, bias=negm[:]))
    for c in range(nch):
        w0 = 512 * c
        w1 = min(W, w0 + 512)
        p.s(lambda e, b=buf, w0=w0, w1=w1, c=c: e.activation(
            out=sqscr[:, 0:w1 - w0], in_=b[:, w0:w1], func=AF.Square,
            accum_out=accc[:, c:c + 1]))
    p.v(lambda e, n=nch: e.reduce_sum(out=sumc[:], in_=accc[:, 0:n], axis=AX))
    p.s(lambda e: e.activation(out=stdc[:], in_=sumc[:], func=AF.Sqrt, scale=1.0 / W, bias=epsc[:]))
    p.v(lambda e: e.reciprocal(out=stdc[:], in_=stdc[:]))
    p.v(lambda e, b=buf: e.tensor_scalar_mul(out=b, in0=b, scalar1=stdc[:]))


def run(build_fn, in_maps):
    nc = bass.Bass("TRN2", target_bir_lowering=False)
    build_fn(nc)
    res = run_bass_kernel_spmd(nc, in_maps, core_ids=list(range(NCORE)))
    return res.results


def f32c(a):
    return np.ascontiguousarray(a, dtype=np.float32)


def bfc(a):
    return np.ascontiguousarray(a)


def build_scan(nc):
    xs = nc.dram_tensor("xs", [B, T, 128], F32, kind="ExternalInput").ap()
    a_s = nc.dram_tensor("a_s", [128, 1], F32, kind="ExternalInput").ap()
    b_s = nc.dram_tensor("b_s", [128, 1], F32, kind="ExternalInput").ap()
    idf = nc.dram_tensor("idf", [128, 128], F32, kind="ExternalInput").ap()
    yT = nc.dram_tensor("yT", [B, 128, T], F32, kind="ExternalOutput").ap()
    with ExitStack() as st:
        xT = st.enter_context(nc.sbuf_tensor("xT", [128, T], F32))
        bx = st.enter_context(nc.sbuf_tensor("bx", [128, T], F32))
        abc = st.enter_context(nc.sbuf_tensor("abc", [128, T], F32))
        acol = st.enter_context(nc.sbuf_tensor("acol", [128, 1], F32))
        bcol = st.enter_context(nc.sbuf_tensor("bcol", [128, 1], F32))
        ident = st.enter_context(nc.sbuf_tensor("ident", [128, 128], F32))
        tin = st.enter_context(nc.sbuf_tensor("tin", [128, 128], F32))
        tp = st.enter_context(nc.psum_tensor("tp", [128, 128], F32))
        S = st.enter_context(nc.semaphore("S"))
        block = st.enter_context(nc.Block())
        p = Prog()
        p.dma(ident[:], idf[:, :])
        p.dma(acol[:], a_s[:, :])
        p.dma(bcol[:], b_s[:, :])
        p.s(lambda e: e.activation(out=acol[:], in_=acol[:], func=AF.Tanh))
        p.v(lambda e: e.memset(abc[:], 1.0))
        p.v(lambda e: e.tensor_scalar_mul(out=abc[:], in0=abc[:], scalar1=acol[:]))
        for b in range(B):
            for j in range(T // 128):
                p.dma(tin[:], xs[b, 128 * j:128 * (j + 1), :])
                p.t(lambda e: e.transpose(out=tp[:], in_=tin[:], identity=ident[:]))
                p.v(lambda e, j=j: e.tensor_copy(out=xT[:, 128 * j:128 * (j + 1)], in_=tp[:]))
            p.v(lambda e: e.tensor_scalar_mul(out=bx[:], in0=xT[:], scalar1=bcol[:]))
            for c in range(4):
                c0 = 2048 * c
                if c == 0:
                    p.v(lambda e: e.tensor_tensor_scan(
                        out=xT[:, 0:2048], data0=abc[:, 0:2048], data1=bx[:, 0:2048],
                        initial=0.0, op0=OP.mult, op1=OP.add))
                else:
                    p.v(lambda e, c0=c0: e.tensor_tensor_scan(
                        out=xT[:, c0:c0 + 2048], data0=abc[:, c0:c0 + 2048],
                        data1=bx[:, c0:c0 + 2048], initial=xT[:, c0 - 1:c0],
                        op0=OP.mult, op1=OP.add))
            p.dma(yT[b, :, :], xT[:])
        p.emit(nc, S, block)


def build_l2(nc):
    xs = nc.dram_tensor("xs", [TPC, D], F32, kind="ExternalInput").ap()
    yTs = nc.dram_tensor("yTs", [D, TPC], F32, kind="ExternalInput").ap()
    w_outT = nc.dram_tensor("w_outT", [D, D], F32, kind="ExternalInput").ap()
    b_row = nc.dram_tensor("b_row", [1, D], F32, kind="ExternalInput").ap()
    w_fcT = nc.dram_tensor("w_fcT", [D, 4096], F32, kind="ExternalInput").ap()
    w_projT = nc.dram_tensor("w_projT", [4096, D], F32, kind="ExternalInput").ap()
    uT = nc.dram_tensor("uT", [16, 4096], F32, kind="ExternalInput").ap()
    vT = nc.dram_tensor("vT", [D, 16], F32, kind="ExternalInput").ap()
    fbT = nc.dram_tensor("fbT", [16, 16], F32, kind="ExternalInput").ap()
    gate = nc.dram_tensor("gate", [1, 1], F32, kind="ExternalInput").ap()
    onesr = nc.dram_tensor("onesr", [1, 128], F32, kind="ExternalInput").ap()
    onesc = nc.dram_tensor("onesc", [128, 1], F32, kind="ExternalInput").ap()
    idf = nc.dram_tensor("idf", [128, 128], F32, kind="ExternalInput").ap()
    w2hbm = nc.dram_tensor("w2hbm", [32, 128, D], BF16, kind="Internal").ap()
    x2o = nc.dram_tensor("x2o", [TPC, D], F32, kind="ExternalOutput").ap()
    with ExitStack() as st:
        sb = lambda n, sh, dt=F32: st.enter_context(nc.sbuf_tensor(n, sh, dt))
        w1e = sb("w1e", [128, 8 * 4096], BF16)
        scbc = sb("scbc", [128, 4096], F32)
        scbc2 = sb("scbc2", [128, 1024], F32)
        shalf = sb("shalf", [128, 1024], F32)
        scrow = sb("scrow", [1, 4096], F32)
        wbuf = sb("wbuf", [128, 1024], F32)
        zb = sb("zb", [128, 1024], F32)
        t1b = sb("t1b", [128, 1024], F32)
        stg = sb("stg", [128, 1024], BF16)
        wo_bf = sb("wo_bf", [128, 8 * 1024], BF16)
        vt_bf = sb("vt_bf", [128, 128], BF16)
        ut_bf = sb("ut_bf", [16, 4096], BF16)
        fb_bf = sb("fb_bf", [16, 16], BF16)
        g16 = sb("g16", [16, 1], F32)
        brow = sb("brow", [1, 1024], F32)
        o1r = sb("o1r", [1, 128], F32)
        o1c = sb("o1c", [128, 1], F32)
        ident = sb("ident", [128, 128], F32)
        gsb = sb("gsb", [1, 1], F32)
        X = sb("X", [128, 2048], F32)
        X1 = sb("X1", [128, 2048], F32)
        ytf = sb("ytf", [128, 2048], F32)
        yt_bf = sb("yt_bf", [128, 2048], BF16)
        x1T = sb("x1T", [128, 2048], BF16)
        rsq = sb("rsq", [128, 8192], BF16)
        rl = sb("rl", [128, 256], F32)
        t1sb = sb("t1sb", [16, 256], BF16)
        t2s = sb("t2s", [16, 256], BF16)
        w2buf = sb("w2buf", [128, 1024], BF16)
        sumc = sb("sumc", [128, 1], F32)
        negm = sb("negm", [128, 1], F32)
        stdc = sb("stdc", [128, 1], F32)
        sqscr = sb("sqscr", [128, 512], F32)
        accc = sb("accc", [128, 4], F32)
        epsc = sb("epsc", [128, 1], F32)
        ps = lambda n, sh, dt=F32: st.enter_context(nc.psum_tensor(n, sh, dt))
        P0 = ps("P0", [128, 512]); P1 = ps("P1", [128, 512])
        P2 = ps("P2", [128, 512]); P3 = ps("P3", [128, 512])
        Pt = ps("Pt", [128, 128]); Plr = ps("Plr", [16, 256])
        P2lr = ps("P2lr", [16, 256]); Ph = ps("Ph", [128, 512])
        POUT = [P0, P1, P2, P3]
        S = st.enter_context(nc.semaphore("S"))
        block = st.enter_context(nc.Block())
        p = Prog()
        p.v(lambda e: e.memset(epsc[:], 1e-5))
        p.dma(ident[:], idf[:, :])
        p.dma(o1r[:], onesr[:, :])
        p.dma(o1c[:], onesc[:, :])
        p.dma(brow[:], b_row[:, :])
        p.dma(gsb[:], gate[:, :])
        p.t(lambda e: e.matmul(out=Plr[0:16, 0:1], lhsT=o1r[0:1, 0:16], rhs=gsb[0:1, 0:1],
                               start=True, stop=True, skip_group_check=True))
        p.v(lambda e: e.tensor_copy(out=g16[:], in_=Plr[0:16, 0:1]))
        for c8 in range(8):
            for k in range(8):
                p.dma(wbuf[:, 0:512], w_fcT[128 * k:128 * (k + 1), 512 * c8:512 * (c8 + 1)])
                p.s(lambda e: e.activation(out=zb[:, 0:512], in_=wbuf[:, 0:512], func=AF.Abs))
                p.t(lambda e, k=k: e.matmul(out=P0[0:1, :], lhsT=o1c[:, 0:1], rhs=zb[:, 0:512],
                                            start=(k == 0), stop=(k == 7), skip_group_check=True))
            p.s(lambda e, c8=c8: e.activation(out=scrow[0:1, 512 * c8:512 * (c8 + 1)],
                                              in_=P0[0:1, :], func=AF.Identity, scale=1.0 / 1024))
        p.v(lambda e: e.tensor_scalar_max(out=scrow[:], in0=scrow[:], scalar1=1e-5))
        for c8 in range(8):
            p.t(lambda e, c8=c8: e.matmul(out=P0[:], lhsT=o1r[0:1, :],
                                          rhs=scrow[0:1, 512 * c8:512 * (c8 + 1)],
                                          start=True, stop=True, skip_group_check=True))
            p.v(lambda e, c8=c8: e.tensor_copy(out=scbc[:, 512 * c8:512 * (c8 + 1)], in_=P0[:]))
        for k in range(8):
            for c in range(4):
                cc = 1024 * c
                p.dma(wbuf[:], w_fcT[128 * k:128 * (k + 1), cc:cc + 1024])
                p.v(lambda e, cc=cc: e.tensor_scalar_mul(out=shalf[:], in0=scbc[:, cc:cc + 1024],
                                                         scalar1=0.5))
                p.s(lambda e: e.activation(out=zb[:], in_=wbuf[:], func=AF.Abs))
                p.v(lambda e: e.tensor_tensor(out=t1b[:], in0=zb[:], in1=shalf[:], op=OP.is_ge))
                p.s(lambda e: e.activation(out=zb[:], in_=wbuf[:], func=AF.Sign))
                p.v(lambda e: e.tensor_tensor(out=t1b[:], in0=t1b[:], in1=zb[:], op=OP.mult))
                p.v(lambda e, k=k, cc=cc: e.tensor_tensor(
                    out=w1e[:, 4096 * k + cc:4096 * k + cc + 1024], in0=t1b[:],
                    in1=scbc[:, cc:cc + 1024], op=OP.mult))
        for c2 in range(2):
            for k2 in range(32):
                p.dma(wbuf[:, 0:512], w_projT[128 * k2:128 * (k2 + 1), 512 * c2:512 * (c2 + 1)])
                p.s(lambda e: e.activation(out=zb[:, 0:512], in_=wbuf[:, 0:512], func=AF.Abs))
                p.t(lambda e, k2=k2: e.matmul(out=P0[0:1, :], lhsT=o1c[:, 0:1], rhs=zb[:, 0:512],
                                              start=(k2 == 0), stop=(k2 == 31), skip_group_check=True))
            p.s(lambda e, c2=c2: e.activation(out=scrow[0:1, 512 * c2:512 * (c2 + 1)],
                                              in_=P0[0:1, :], func=AF.Identity, scale=1.0 / 4096))
        p.v(lambda e: e.tensor_scalar_max(out=scrow[:, 0:1024], in0=scrow[:, 0:1024], scalar1=1e-5))
        for c2 in range(2):
            p.t(lambda e, c2=c2: e.matmul(out=P0[:], lhsT=o1r[0:1, :],
                                          rhs=scrow[0:1, 512 * c2:512 * (c2 + 1)],
                                          start=True, stop=True, skip_group_check=True))
            p.v(lambda e, c2=c2: e.tensor_copy(out=scbc2[:, 512 * c2:512 * (c2 + 1)], in_=P0[:]))
        p.v(lambda e: e.tensor_scalar_mul(out=shalf[:], in0=scbc2[:], scalar1=0.5))
        for k2 in range(32):
            p.dma(wbuf[:], w_projT[128 * k2:128 * (k2 + 1), :])
            p.s(lambda e: e.activation(out=zb[:], in_=wbuf[:], func=AF.Abs))
            p.v(lambda e: e.tensor_tensor(out=t1b[:], in0=zb[:], in1=shalf[:], op=OP.is_ge))
            p.s(lambda e: e.activation(out=zb[:], in_=wbuf[:], func=AF.Sign))
            p.v(lambda e: e.tensor_tensor(out=t1b[:], in0=t1b[:], in1=zb[:], op=OP.mult))
            p.v(lambda e: e.tensor_tensor(out=stg[:], in0=t1b[:], in1=scbc2[:], op=OP.mult))
            p.dma(w2hbm[k2, :, :], stg[:])
        for k in range(8):
            p.dma(wbuf[:], w_outT[128 * k:128 * (k + 1), :])
            p.v(lambda e, k=k: e.tensor_copy(out=wo_bf[:, 1024 * k:1024 * (k + 1)], in_=wbuf[:]))
        for k in range(8):
            p.dma(wbuf[:, 0:16], vT[128 * k:128 * (k + 1), :])
            p.v(lambda e, k=k: e.tensor_copy(out=vt_bf[:, 16 * k:16 * (k + 1)], in_=wbuf[:, 0:16]))
        for c in range(4):
            p.dma(wbuf[0:16, :], uT[:, 1024 * c:1024 * (c + 1)])
            p.v(lambda e, c=c: e.tensor_copy(out=ut_bf[:, 1024 * c:1024 * (c + 1)],
                                             in_=wbuf[0:16, :]))
        p.dma(wbuf[0:16, 0:16], fbT[:, :])
        p.v(lambda e: e.tensor_copy(out=fb_bf[:], in_=wbuf[0:16, 0:16]))
        for s8 in range(8):
            t0 = 256 * s8
            for tl in range(2):
                p.dma(X[:, 1024 * tl:1024 * (tl + 1)], xs[t0 + 128 * tl:t0 + 128 * (tl + 1), :])
            for k in range(8):
                p.dma(ytf[:, 256 * k:256 * (k + 1)], yTs[128 * k:128 * (k + 1), t0:t0 + 256])
            p.v(lambda e: e.tensor_copy(out=yt_bf[:], in_=ytf[:]))
            for tl in range(2):
                for E in range(2):
                    PM = POUT[2 * tl + E]
                    for k in range(8):
                        p.t(lambda e, PM=PM, k=k, tl=tl, E=E: e.matmul(
                            out=PM[:], lhsT=yt_bf[:, 256 * k + 128 * tl:256 * k + 128 * tl + 128],
                            rhs=wo_bf[:, 1024 * k + 512 * E:1024 * k + 512 * E + 512],
                            start=(k == 0), stop=False, skip_group_check=True))
                    p.t(lambda e, PM=PM, E=E: e.matmul(
                        out=PM[:], lhsT=o1r[0:1, :], rhs=brow[0:1, 512 * E:512 * E + 512],
                        start=False, stop=True, skip_group_check=True))
                    p.v(lambda e, PM=PM, tl=tl, E=E: e.tensor_tensor(
                        out=X1[:, 1024 * tl + 512 * E:1024 * tl + 512 * E + 512],
                        in0=X[:, 1024 * tl + 512 * E:1024 * tl + 512 * E + 512],
                        in1=PM[:], op=OP.add))
                ln_plain(p, X1[:, 1024 * tl:1024 * (tl + 1)], 1024, sumc, negm, stdc, sqscr, accc, epsc)
                for k in range(8):
                    p.t(lambda e, tl=tl, k=k: e.transpose(
                        out=Pt[:], in_=X1[:, 1024 * tl + 128 * k:1024 * tl + 128 * k + 128],
                        identity=ident[:]))
                    p.v(lambda e, tl=tl, k=k: e.tensor_copy(
                        out=x1T[:, 256 * k + 128 * tl:256 * k + 128 * tl + 128], in_=Pt[:]))
            for k in range(8):
                p.t(lambda e, k=k: e.matmul(out=Plr[:], lhsT=vt_bf[:, 16 * k:16 * (k + 1)],
                                            rhs=x1T[:, 256 * k:256 * (k + 1)],
                                            start=(k == 0), stop=(k == 7), skip_group_check=True))
            p.v(lambda e: e.tensor_copy(out=t1sb[:], in_=Plr[:]))
            p.t(lambda e: e.matmul(out=P2lr[:], lhsT=fb_bf[:], rhs=t1sb[:],
                                   start=True, stop=True, skip_group_check=True))
            p.v(lambda e: e.tensor_scalar_mul(out=t2s[:], in0=P2lr[:], scalar1=g16[:]))
            for H in range(32):
                for k in range(8):
                    p.t(lambda e, H=H, k=k: e.matmul(
                        out=Ph[:, 0:256], lhsT=w1e[:, 4096 * k + 128 * H:4096 * k + 128 * H + 128],
                        rhs=x1T[:, 256 * k:256 * (k + 1)],
                        start=(k == 0), stop=False, skip_group_check=True))
                p.t(lambda e, H=H: e.matmul(out=Ph[:, 0:256], lhsT=ut_bf[0:16, 128 * H:128 * H + 128],
                                            rhs=t2s[:], start=False, stop=True, skip_group_check=True))
                p.s(lambda e: e.activation(out=rl[:], in_=Ph[:, 0:256], func=AF.Relu))
                p.s(lambda e, H=H: e.activation(out=rsq[:, 256 * H:256 * (H + 1)], in_=rl[:],
                                                func=AF.Square))
            for k2 in range(32):
                p.dma(w2buf[:], w2hbm[k2, :, :])
                for tl in range(2):
                    for E in range(2):
                        p.t(lambda e, k2=k2, tl=tl, E=E: e.matmul(
                            out=POUT[2 * tl + E][:],
                            lhsT=rsq[:, 256 * k2 + 128 * tl:256 * k2 + 128 * tl + 128],
                            rhs=w2buf[:, 512 * E:512 * E + 512],
                            start=(k2 == 0), stop=(k2 == 31), skip_group_check=True))
            for tl in range(2):
                for E in range(2):
                    p.v(lambda e, tl=tl, E=E: e.tensor_tensor(
                        out=X[:, 1024 * tl + 512 * E:1024 * tl + 512 * E + 512],
                        in0=X1[:, 1024 * tl + 512 * E:1024 * tl + 512 * E + 512],
                        in1=POUT[2 * tl + E][:], op=OP.add))
                ln_plain(p, X[:, 1024 * tl:1024 * (tl + 1)], 1024, sumc, negm, stdc, sqscr, accc, epsc)
                p.dma(x2o[t0 + 128 * tl:t0 + 128 * (tl + 1), :], X[:, 1024 * tl:1024 * (tl + 1)])
        p.emit(nc, S, block)


def build_l3(nc):
    QKVP, HID = 7168, 1024
    x2h = nc.dram_tensor("x2h", [TPC + 128, D], F32, kind="ExternalInput").ap()
    halo = nc.dram_tensor("halo", [128, 1], F32, kind="ExternalInput").ap()
    g_in = nc.dram_tensor("g_in", [D, 1], F32, kind="ExternalInput").ap()
    b_in = nc.dram_tensor("b_in", [D, 1], F32, kind="ExternalInput").ap()
    winT = nc.dram_tensor("winT", [D, QKVP], F32, kind="ExternalInput").ap()
    mg = nc.dram_tensor("mg", [1, QKVP], F32, kind="ExternalInput").ap()
    mb = nc.dram_tensor("mb", [1, QKVP], F32, kind="ExternalInput").ap()
    onesr = nc.dram_tensor("onesr", [1, 128], F32, kind="ExternalInput").ap()
    idf = nc.dram_tensor("idf", [128, 128], F32, kind="ExternalInput").ap()
    qkvo = nc.dram_tensor("qkvo", [TPC, 3072], BF16, kind="ExternalOutput").ap()
    pgo = nc.dram_tensor("pgo", [TPC, 4096], BF16, kind="ExternalOutput").ap()
    with ExitStack() as st:
        sb = lambda n, sh, dt=F32: st.enter_context(nc.sbuf_tensor(n, sh, dt))
        win_bf = sb("win_bf", [128, 8 * QKVP], BF16)
        mgbc = sb("mgbc", [128, QKVP], BF16)
        mbbc = sb("mbbc", [128, QKVP], BF16)
        h_sb = sb("h_sb", [128, QKVP], F32)
        gcols = sb("gcols", [128, 8], F32)
        bcols = sb("bcols", [128, 8], F32)
        bhcols = sb("bhcols", [128, 8], F32)
        halocol = sb("halocol", [128, 1], F32)
        xt = sb("xt", [128, 1024], F32)
        xbuf = sb("xbuf", [128, 2048], BF16)
        egc = sb("egc", [128, 512], F32)
        pgb = sb("pgb", [128, 4096], BF16)
        o1r = sb("o1r", [1, 128], F32)
        ident = sb("ident", [128, 128], F32)
        sumc = sb("sumc", [128, 1], F32)
        negm = sb("negm", [128, 1], F32)
        stdc = sb("stdc", [128, 1], F32)
        sqscr = sb("sqscr", [128, 512], F32)
        accc = sb("accc", [128, 16], F32)
        epsc = sb("epsc", [128, 1], F32)
        Pm = st.enter_context(nc.psum_tensor("Pm", [128, 512], F32))
        Pt = st.enter_context(nc.psum_tensor("Pt", [128, 128], F32))
        S = st.enter_context(nc.semaphore("S"))
        block = st.enter_context(nc.Block())
        p = Prog()
        p.v(lambda e: e.memset(epsc[:], 1e-5))
        p.dma(ident[:], idf[:, :])
        p.dma(o1r[:], onesr[:, :])
        p.dma(halocol[:], halo[:, :])
        for k in range(8):
            p.dma(gcols[:, k:k + 1], g_in[128 * k:128 * (k + 1), :])
            p.dma(bcols[:, k:k + 1], b_in[128 * k:128 * (k + 1), :])
        p.v(lambda e: e.tensor_scalar_mul(out=bhcols[:], in0=bcols[:], scalar1=halocol[:]))
        for src, dst in ((mg, mgbc), (mb, mbbc)):
            p.dma(h_sb[0:1, :], src[:, :])
            for c in range(14):
                p.t(lambda e, c=c: e.matmul(out=Pm[:], lhsT=o1r[0:1, :],
                                            rhs=h_sb[0:1, 512 * c:512 * (c + 1)],
                                            start=True, stop=True, skip_group_check=True))
                p.v(lambda e, dst=dst, c=c: e.tensor_copy(out=dst[:, 512 * c:512 * (c + 1)], in_=Pm[:]))
        for k in range(8):
            p.dma(h_sb[:], winT[128 * k:128 * (k + 1), :])
            p.v(lambda e, k=k: e.tensor_copy(out=win_bf[:, QKVP * k:QKVP * (k + 1)], in_=h_sb[:]))
        for tau in range(-1, 16):
            r0 = 128 * (tau + 1)
            p.dma(xt[:], x2h[r0:r0 + 128, :])
            if tau == -1:
                p.v(lambda e: e.tensor_scalar_mul(out=xt[:], in0=xt[:], scalar1=halocol[:]))
            ln_plain(p, xt[:], 1024, sumc, negm, stdc, sqscr, accc, epsc)
            if tau >= 0:
                for k in range(8):
                    p.v(lambda e, k=k: e.tensor_copy(out=xbuf[:, 256 * k:256 * k + 128],
                                                     in_=xbuf[:, 256 * k + 128:256 * k + 256]))
            bsel = bhcols if tau == -1 else bcols
            for k in range(8):
                p.t(lambda e, k=k: e.transpose(out=Pt[:], in_=xt[:, 128 * k:128 * (k + 1)],
                                               identity=ident[:]))
                p.v(lambda e, k=k, bsel=bsel: e.tensor_scalar(
                    out=xbuf[:, 256 * k + 128:256 * k + 256], in0=Pt[:],
                    scalar1=gcols[:, k:k + 1], scalar2=bsel[:, k:k + 1],
                    op0=OP.mult, op1=OP.add))
            if tau < 0:
                continue
            shifts = [1, 3, 0, 0, 0, 0, 0, 0]
            for O in range(14):
                for k in range(8):
                    off = 256 * k + 128 - shifts[k]
                    p.t(lambda e, O=O, k=k, off=off: e.matmul(
                        out=Pm[:], lhsT=xbuf[:, off:off + 128],
                        rhs=win_bf[:, QKVP * k + 512 * O:QKVP * k + 512 * O + 512],
                        start=(k == 0), stop=(k == 7), skip_group_check=True))
                p.s(lambda e, O=O: e.activation(out=h_sb[:, 512 * O:512 * (O + 1)], in_=Pm[:],
                                                func=AF.Relu, accum_out=accc[:, O:O + 1]))
            p.v(lambda e: e.reduce_sum(out=sumc[:], in_=accc[:, 0:14], axis=AX))
            p.v(lambda e: e.tensor_scalar_mul(out=negm[:], in0=sumc[:], scalar1=-1.0 / QKVP))
            p.s(lambda e: e.activation(out=h_sb[:], in_=h_sb[:], func=AF.Identity, bias=negm[:]))
            for c in range(14):
                p.s(lambda e, c=c: e.activation(out=sqscr[:], in_=h_sb[:, 512 * c:512 * (c + 1)],
                                                func=AF.Square, accum_out=accc[:, c:c + 1]))
            p.v(lambda e: e.reduce_sum(out=sumc[:], in_=accc[:, 0:14], axis=AX))
            p.s(lambda e: e.activation(out=stdc[:], in_=sumc[:], func=AF.Sqrt,
                                       scale=1.0 / QKVP, bias=epsc[:]))
            p.v(lambda e: e.reciprocal(out=stdc[:], in_=stdc[:]))
            p.v(lambda e: e.tensor_scalar_mul(out=h_sb[:], in0=h_sb[:], scalar1=stdc[:]))
            p.v(lambda e: e.tensor_tensor(out=h_sb[:], in0=h_sb[:], in1=mgbc[:], op=OP.mult))
            p.v(lambda e: e.tensor_tensor(out=h_sb[:], in0=h_sb[:], in1=mbbc[:], op=OP.add))
            p.v(lambda e: e.tensor_copy(out=pgb[:, 0:3072], in_=h_sb[:, 0:3072]))
            p.dma(qkvo[128 * tau:128 * (tau + 1), :], pgb[:, 0:3072])
            for c in range(8):
                pc = 3072 + 512 * c
                p.s(lambda e, pc=pc: e.activation(out=egc[:], in_=h_sb[:, pc:pc + 512],
                                                  func=AF.Erf, scale=0.7071067811865476))
                p.v(lambda e: e.tensor_scalar(out=egc[:], in0=egc[:], scalar1=0.5, scalar2=0.5,
                                              op0=OP.mult, op1=OP.add))
                p.v(lambda e, pc=pc: e.tensor_tensor(out=egc[:], in0=egc[:],
                                                     in1=h_sb[:, pc:pc + 512], op=OP.mult))
                if c == 0:
                    p.v(lambda e: e.tensor_tensor(out=pgb[:, 1:512], in0=egc[:, 1:512],
                                                  in1=h_sb[:, 3072:3072 + 511], op=OP.mult))
                    p.v(lambda e: e.tensor_tensor(out=pgb[:, 0:1], in0=egc[:, 0:1],
                                                  in1=h_sb[:, 7167:7168], op=OP.mult))
                else:
                    p.v(lambda e, c=c, pc=pc: e.tensor_tensor(
                        out=pgb[:, 512 * c:512 * (c + 1)], in0=egc[:],
                        in1=h_sb[:, pc - 1:pc + 511], op=OP.mult))
            p.dma(pgo[128 * tau:128 * (tau + 1), :], pgb[:])
        p.emit(nc, S, block)


def build_l4(nc):
    L = T
    q_in = nc.dram_tensor("q_in", [4, L, 64], BF16, kind="ExternalInput").ap()
    k_in = nc.dram_tensor("k_in", [4, L, 64], BF16, kind="ExternalInput").ap()
    v_in = nc.dram_tensor("v_in", [4, L, 64], BF16, kind="ExternalInput").ap()
    sl_in = nc.dram_tensor("sl_in", [4, 1], F32, kind="ExternalInput").ap()
    iota = nc.dram_tensor("iota", [128, 64], F32, kind="ExternalInput").ap()
    caus = nc.dram_tensor("caus", [128, 2048], F32, kind="ExternalInput").ap()
    idb = nc.dram_tensor("idb", [128, 128], BF16, kind="ExternalInput").ap()
    onesr = nc.dram_tensor("onesr", [1, 128], F32, kind="ExternalInput").ap()
    oT = nc.dram_tensor("oT", [4, 64, L], BF16, kind="ExternalOutput").ap()
    with ExitStack() as st:
        sb = lambda n, sh, dt=F32: st.enter_context(nc.sbuf_tensor(n, sh, dt))
        qT = sb("qT", [64, L], BF16)
        kT = sb("kT", [64, L], BF16)
        vaug = sb("vaug", [128, 65 * 64], BF16)
        qbuf = sb("qbuf", [128, 4096], BF16)
        ssb = sb("ssb", [128, 512], F32)
        pbf = sb("pbf", [128, 512], BF16)
        Macc = sb("Macc", [128, 512], F32)
        mred = sb("mred", [64, 512], F32)
        negr = sb("negr", [1, 512], F32)
        nbc = sb("nbc", [128, 512], F32)
        alall = sb("alall", [128, 64], F32)
        iot = sb("iot", [128, 64], F32)
        casb = sb("casb", [128, 2048], F32)
        o1r = sb("o1r", [1, 128], F32)
        idsb = sb("idsb", [128, 128], BF16)
        slsb = sb("slsb", [1, 1], F32)
        slcol = sb("slcol", [128, 1], F32)
        of32 = sb("of32", [64, 512], F32)
        obf = sb("obf", [64, 512], BF16)
        rrow = sb("rrow", [1, 512], F32)
        ps_s = st.enter_context(nc.psum_tensor("ps_s", [128, 512], F32))
        ps_o = st.enter_context(nc.psum_tensor("ps_o", [65, 512], F32))
        ps_b = st.enter_context(nc.psum_tensor("ps_b", [128, 512], F32))
        ps_t = st.enter_context(nc.psum_tensor("ps_t", [64, 128], BF16))
        ps_r = st.enter_context(nc.psum_tensor("ps_r", [64, 512], F32))
        ps_c = st.enter_context(nc.psum_tensor("ps_c", [128, 1], F32))
        S = st.enter_context(nc.semaphore("S"))
        block = st.enter_context(nc.Block())
        p = Prog()
        p.dma(iot[:], iota[:, :])
        p.dma(casb[:], caus[:, :])
        p.dma(idsb[:], idb[:, :])
        p.dma(o1r[:], onesr[:, :])
        for pi in range(4):
            p.dma(slsb[:], sl_in[pi:pi + 1, :])
            p.t(lambda e: e.matmul(out=ps_c[:], lhsT=o1r[0:1, :], rhs=slsb[0:1, 0:1],
                                   start=True, stop=True, skip_group_check=True))
            p.v(lambda e: e.tensor_copy(out=slcol[:], in_=ps_c[:]))
            p.v(lambda e: e.tensor_scalar_mul(out=alall[:], in0=iot[:], scalar1=slcol[:]))
            p.v(lambda e: e.memset(vaug[:], 1.0))
            for g in range(8):
                p.dma(vaug[:].rearrange("p (j c) -> p j c", c=65)[:, 8 * g:8 * (g + 1), 0:64],
                      v_in[pi].rearrange("(j p) c -> p j c", p=128)[:, 8 * g:8 * (g + 1), :])
            for src, dstT in ((q_in, qT), (k_in, kT)):
                for g in range(2):
                    p.dma(qbuf[:].rearrange("p (j c) -> p j c", c=64)[:, 0:32, :],
                          src[pi].rearrange("(j p) c -> p j c", p=128)[:, 32 * g:32 * (g + 1), :])
                    for jj in range(32):
                        j = 32 * g + jj
                        p.t(lambda e, jj=jj: e.transpose(out=ps_t[:], in_=qbuf[:, 64 * jj:64 * (jj + 1)],
                                                         identity=idsb[:]))
                        p.v(lambda e, j=j, dstT=dstT: e.tensor_copy(
                            out=dstT[:, 128 * j:128 * (j + 1)], in_=ps_t[:]))
            for Q in range(16):
                q0 = 512 * Q
                nk = 4 * Q + 4
                for j in range(nk):
                    p.t(lambda e, j=j, q0=q0: e.matmul(
                        out=ps_s[:], lhsT=kT[:, 128 * j:128 * (j + 1)], rhs=qT[:, q0:q0 + 512],
                        start=True, stop=True, skip_group_check=True))
                    p.s(lambda e, j=j: e.activation(out=ssb[:], in_=ps_s[:], func=AF.Identity,
                                                    bias=alall[:, j:j + 1], scale=0.125))
                    if j >= 4 * Q:
                        v_ = j - 4 * Q
                        p.v(lambda e, v_=v_: e.tensor_tensor(
                            out=ssb[:], in0=ssb[:], in1=casb[:, 512 * v_:512 * (v_ + 1)], op=OP.add))
                    if j == 0:
                        p.v(lambda e: e.tensor_copy(out=Macc[:], in_=ssb[:]))
                    else:
                        p.v(lambda e: e.tensor_tensor(out=Macc[:], in0=Macc[:], in1=ssb[:], op=OP.max))
                for sh in (64, 32, 16, 8, 4, 2, 1):
                    p.dma(mred[0:sh, :], Macc[sh:2 * sh, :])
                    p.v(lambda e, sh=sh: e.tensor_tensor(out=Macc[0:sh, :], in0=Macc[0:sh, :],
                                                         in1=mred[0:sh, :], op=OP.max))
                p.v(lambda e: e.tensor_scalar_mul(out=negr[:], in0=Macc[0:1, :], scalar1=-1.0))
                p.t(lambda e: e.matmul(out=ps_b[:], lhsT=o1r[0:1, :], rhs=negr[0:1, :],
                                       start=True, stop=True, skip_group_check=True))
                p.v(lambda e: e.tensor_copy(out=nbc[:], in_=ps_b[:]))
                for j in range(nk):
                    p.t(lambda e, j=j, q0=q0: e.matmul(
                        out=ps_s[:], lhsT=kT[:, 128 * j:128 * (j + 1)], rhs=qT[:, q0:q0 + 512],
                        start=True, stop=True, skip_group_check=True))
                    p.s(lambda e, j=j: e.activation(out=ssb[:], in_=ps_s[:], func=AF.Identity,
                                                    bias=alall[:, j:j + 1], scale=0.125))
                    if j >= 4 * Q:
                        v_ = j - 4 * Q
                        p.v(lambda e, v_=v_: e.tensor_tensor(
                            out=ssb[:], in0=ssb[:], in1=casb[:, 512 * v_:512 * (v_ + 1)], op=OP.add))
                    p.v(lambda e: e.tensor_tensor(out=ssb[:], in0=ssb[:], in1=nbc[:], op=OP.add))
                    p.s(lambda e: e.activation(out=pbf[:], in_=ssb[:], func=AF.Exp))
                    p.t(lambda e, j=j, nk=nk: e.matmul(
                        out=ps_o[:], lhsT=vaug[:, 65 * j:65 * (j + 1)], rhs=pbf[:],
                        start=(j == 0), stop=(j == nk - 1), skip_group_check=True))
                p.v(lambda e: e.reciprocal(out=rrow[:], in_=ps_o[64:65, :]))
                p.t(lambda e: e.matmul(out=ps_r[:], lhsT=o1r[0:1, 0:64], rhs=rrow[0:1, :],
                                       start=True, stop=True, skip_group_check=True))
                p.v(lambda e: e.tensor_copy(out=of32[:], in_=ps_o[0:64, :]))
                p.v(lambda e: e.tensor_tensor(out=obf[:], in0=of32[:], in1=ps_r[:], op=OP.mult))
                p.dma(oT[pi, :, q0:q0 + 512], obf[:])
        p.emit(nc, S, block)


def build_l5(nc):
    VP = 5120
    oTs = nc.dram_tensor("oTs", [D, TPC], BF16, kind="ExternalInput").ap()
    pgs = nc.dram_tensor("pgs", [TPC, 4096], BF16, kind="ExternalInput").ap()
    x2s = nc.dram_tensor("x2s", [TPC, D], F32, kind="ExternalInput").ap()
    woT = nc.dram_tensor("woT", [VP, D], F32, kind="ExternalInput").ap()
    og = nc.dram_tensor("og", [1, D], F32, kind="ExternalInput").ap()
    ob = nc.dram_tensor("ob", [1, D], F32, kind="ExternalInput").ap()
    gw = nc.dram_tensor("gw", [D, 8], F32, kind="ExternalInput").ap()
    w0T = nc.dram_tensor("w0T", [D, D], F32, kind="ExternalInput").ap()
    b0r = nc.dram_tensor("b0r", [1, D], F32, kind="ExternalInput").ap()
    w1T = nc.dram_tensor("w1T", [D, D], F32, kind="ExternalInput").ap()
    b1r = nc.dram_tensor("b1r", [1, D], F32, kind="ExternalInput").ap()
    onesr = nc.dram_tensor("onesr", [1, 128], F32, kind="ExternalInput").ap()
    idf = nc.dram_tensor("idf", [128, 128], F32, kind="ExternalInput").ap()
    idb = nc.dram_tensor("idb", [128, 128], BF16, kind="ExternalInput").ap()
    yo = nc.dram_tensor("yo", [TPC, D], F32, kind="ExternalOutput").ap()
    with ExitStack() as st:
        sb = lambda n, sh, dt=F32: st.enter_context(nc.sbuf_tensor(n, sh, dt))
        wo_bf = sb("wo_bf", [128, 40 * 1024], BF16)
        gw_bf = sb("gw_bf", [128, 64], BF16)
        w0_bf = sb("w0_bf", [128, 8192], BF16)
        w1_bf = sb("w1_bf", [128, 8192], BF16)
        ogbc = sb("ogbc", [128, 1024], F32)
        obbc = sb("obbc", [128, 1024], F32)
        b0sb = sb("b0sb", [1, 1024], F32)
        b1sb = sb("b1sb", [1, 1024], F32)
        wbuf = sb("wbuf", [128, 1024], F32)
        o1r = sb("o1r", [1, 128], F32)
        identf = sb("identf", [128, 128], F32)
        identb = sb("identb", [128, 128], BF16)
        xt = sb("xt", [128, 1024], F32)
        x3 = sb("x3", [128, 1024], F32)
        obuf = sb("obuf", [128, 1024], F32)
        ebuf = sb("ebuf", [128, 1024], F32)
        x3T = sb("x3T", [128, 1024], BF16)
        otb = sb("otb", [128, 1024], BF16)
        pgt = sb("pgt", [128, 4096], BF16)
        pgT = sb("pgT", [128, 4096], BF16)
        wsm = sb("wsm", [128, 8], F32)
        gb8 = sb("gb8", [128, 8], F32)
        mxc = sb("mxc", [128, 1], F32)
        nmc = sb("nmc", [128, 1], F32)
        smc = sb("smc", [128, 1], F32)
        rsc = sb("rsc", [128, 1], F32)
        rmc = sb("rmc", [128, 1], F32)
        m0c = sb("m0c", [128, 1], F32)
        m1c = sb("m1c", [128, 1], F32)
        g0c = sb("g0c", [128, 1], F32)
        g1c = sb("g1c", [128, 1], F32)
        ccol = sb("ccol", [128, 1], F32)
        sumc = sb("sumc", [128, 1], F32)
        negm = sb("negm", [128, 1], F32)
        stdc = sb("stdc", [128, 1], F32)
        sqscr = sb("sqscr", [128, 512], F32)
        accc = sb("accc", [128, 4], F32)
        epsc = sb("epsc", [128, 1], F32)
        PA = st.enter_context(nc.psum_tensor("PA", [128, 512], F32))
        PB = st.enter_context(nc.psum_tensor("PB", [128, 512], F32))
        Ptf = st.enter_context(nc.psum_tensor("Ptf", [128, 128], F32))
        Ptb = st.enter_context(nc.psum_tensor("Ptb", [128, 128], BF16))
        Pg = st.enter_context(nc.psum_tensor("Pg", [128, 8], F32))
        S = st.enter_context(nc.semaphore("S"))
        block = st.enter_context(nc.Block())
        p = Prog()
        p.v(lambda e: e.memset(epsc[:], 1e-5))
        p.dma(identf[:], idf[:, :])
        p.dma(identb[:], idb[:, :])
        p.dma(o1r[:], onesr[:, :])
        p.dma(b0sb[:], b0r[:, :])
        p.dma(b1sb[:], b1r[:, :])
        for k in range(40):
            p.dma(wbuf[:], woT[128 * k:128 * (k + 1), :])
            p.v(lambda e, k=k: e.tensor_copy(out=wo_bf[:, 1024 * k:1024 * (k + 1)], in_=wbuf[:]))
        for k in range(8):
            p.dma(wbuf[:, 0:8], gw[128 * k:128 * (k + 1), :])
            p.v(lambda e, k=k: e.tensor_copy(out=gw_bf[:, 8 * k:8 * (k + 1)], in_=wbuf[:, 0:8]))
        for src, dst in ((w0T, w0_bf), (w1T, w1_bf)):
            for k in range(8):
                p.dma(wbuf[:], src[128 * k:128 * (k + 1), :])
                p.v(lambda e, dst=dst, k=k: e.tensor_copy(
                    out=dst[:, 1024 * k:1024 * (k + 1)], in_=wbuf[:]))
        for src, dst in ((og, ogbc), (ob, obbc)):
            p.dma(wbuf[0:1, :], src[:, :])
            for c in range(2):
                p.t(lambda e, c=c: e.matmul(out=PA[:], lhsT=o1r[0:1, :],
                                            rhs=wbuf[0:1, 512 * c:512 * (c + 1)],
                                            start=True, stop=True, skip_group_check=True))
                p.v(lambda e, dst=dst, c=c: e.tensor_copy(out=dst[:, 512 * c:512 * (c + 1)], in_=PA[:]))
        for tau in range(16):
            r0 = 128 * tau
            for k in range(8):
                p.dma(otb[:, 128 * k:128 * (k + 1)], oTs[128 * k:128 * (k + 1), r0:r0 + 128])
            p.dma(pgt[:], pgs[r0:r0 + 128, :])
            for m in range(32):
                p.t(lambda e, m=m: e.transpose(out=Ptb[:], in_=pgt[:, 128 * m:128 * (m + 1)],
                                               identity=identb[:]))
                p.v(lambda e, m=m: e.tensor_copy(out=pgT[:, 128 * m:128 * (m + 1)], in_=Ptb[:]))
            for E in range(2):
                PX = PA if E == 0 else PB
                for k in range(8):
                    p.t(lambda e, PX=PX, k=k, E=E: e.matmul(
                        out=PX[:], lhsT=otb[:, 128 * k:128 * (k + 1)],
                        rhs=wo_bf[:, 1024 * k + 512 * E:1024 * k + 512 * E + 512],
                        start=(k == 0), stop=False, skip_group_check=True))
                for m in range(32):
                    p.t(lambda e, PX=PX, m=m, E=E: e.matmul(
                        out=PX[:], lhsT=pgT[:, 128 * m:128 * (m + 1)],
                        rhs=wo_bf[:, 1024 * (8 + m) + 512 * E:1024 * (8 + m) + 512 * E + 512],
                        start=False, stop=(m == 31), skip_group_check=True))
                p.v(lambda e, PX=PX, E=E: e.tensor_copy(out=obuf[:, 512 * E:512 * E + 512], in_=PX[:]))
            ln_plain(p, obuf[:], 1024, sumc, negm, stdc, sqscr, accc, epsc)
            p.v(lambda e: e.tensor_tensor(out=obuf[:], in0=obuf[:], in1=ogbc[:], op=OP.mult))
            p.v(lambda e: e.tensor_tensor(out=obuf[:], in0=obuf[:], in1=obbc[:], op=OP.add))
            p.dma(xt[:], x2s[r0:r0 + 128, :])
            p.v(lambda e: e.tensor_tensor(out=x3[:], in0=xt[:], in1=obuf[:], op=OP.add))
            ln_plain(p, x3[:], 1024, sumc, negm, stdc, sqscr, accc, epsc)
            for k in range(8):
                p.t(lambda e, k=k: e.transpose(out=Ptf[:], in_=x3[:, 128 * k:128 * (k + 1)],
                                               identity=identf[:]))
                p.v(lambda e, k=k: e.tensor_copy(out=x3T[:, 128 * k:128 * (k + 1)], in_=Ptf[:]))
            for k in range(8):
                p.t(lambda e, k=k: e.matmul(out=Pg[:], lhsT=x3T[:, 128 * k:128 * (k + 1)],
                                            rhs=gw_bf[:, 8 * k:8 * (k + 1)],
                                            start=(k == 0), stop=(k == 7), skip_group_check=True))
            p.v(lambda e: e.reduce_max(out=mxc[:], in_=Pg[:], axis=AX))
            p.v(lambda e: e.tensor_scalar_mul(out=nmc[:], in0=mxc[:], scalar1=-1.0))
            p.s(lambda e: e.activation(out=wsm[:], in_=Pg[:], func=AF.Exp, bias=nmc[:],
                                       accum_out=smc[:]))
            p.v(lambda e: e.reciprocal(out=rsc[:], in_=smc[:]))
            p.v(lambda e: e.tensor_scalar_mul(out=wsm[:], in0=wsm[:], scalar1=rsc[:]))
            p.v(lambda e: e.reduce_max(out=rmc[:], in_=wsm[:, 1:8], axis=AX))
            p.v(lambda e: e.tensor_tensor(out=m0c[:], in0=wsm[:, 0:1], in1=rmc[:], op=OP.is_gt))
            p.v(lambda e: e.tensor_scalar(out=gb8[:], in0=wsm[:], scalar1=wsm[:, 1:2],
                                          scalar2=None, op0=OP.is_gt))
            p.v(lambda e: e.reduce_sum(out=ccol[:], in_=gb8[:], axis=AX))
            p.v(lambda e: e.tensor_scalar(out=m1c[:], in0=ccol[:], scalar1=1.0,
                                          scalar2=None, op0=OP.is_equal))
            p.v(lambda e: e.tensor_tensor(out=g0c[:], in0=wsm[:, 0:1], in1=m0c[:], op=OP.mult))
            p.v(lambda e: e.tensor_tensor(out=g1c[:], in0=wsm[:, 1:2], in1=m1c[:], op=OP.mult))
            for ei, (wbf, bsb, gcolv) in enumerate(((w0_bf, b0sb, g0c), (w1_bf, b1sb, g1c))):
                for E in range(2):
                    PX = PA if E == 0 else PB
                    for k in range(8):
                        p.t(lambda e, PX=PX, wbf=wbf, k=k, E=E: e.matmul(
                            out=PX[:], lhsT=x3T[:, 128 * k:128 * (k + 1)],
                            rhs=wbf[:, 1024 * k + 512 * E:1024 * k + 512 * E + 512],
                            start=(k == 0), stop=False, skip_group_check=True))
                    p.t(lambda e, PX=PX, bsb=bsb, E=E: e.matmul(
                        out=PX[:], lhsT=o1r[0:1, :], rhs=bsb[0:1, 512 * E:512 * E + 512],
                        start=False, stop=True, skip_group_check=True))
                    p.s(lambda e, PX=PX, E=E: e.activation(
                        out=ebuf[:, 512 * E:512 * E + 512], in_=PX[:], func=AF.Silu))
                if ei == 0:
                    p.v(lambda e, gcolv=gcolv: e.tensor_scalar_mul(out=obuf[:], in0=ebuf[:],
                                                                   scalar1=gcolv[:]))
                else:
                    p.v(lambda e, gcolv=gcolv: e.scalar_tensor_tensor(
                        out=obuf[:], in0=ebuf[:], scalar=gcolv[:], in1=obuf[:],
                        op0=OP.mult, op1=OP.add))
            p.v(lambda e: e.tensor_tensor(out=x3[:], in0=x3[:], in1=obuf[:], op=OP.add))
            ln_plain(p, x3[:], 1024, sumc, negm, stdc, sqscr, accc, epsc)
            p.dma(yo[r0:r0 + 128, :], x3[:])
        p.emit(nc, S, block)


def kernel(L0_attn_x, L0_attn_a, L0_attn_b, L0_attn_w_out, L0_attn_b_out,
           L0_ffn_fast_b, L0_ffn_w_fc, L0_ffn_w_proj, L0_ffn_u, L0_ffn_v, L0_ffn_gate,
           L1_attn_in_ln_g, L1_attn_in_ln_b, L1_attn_W_in, L1_attn_mid_ln_g,
           L1_attn_mid_ln_b, L1_attn_slopes, L1_attn_W_out, L1_attn_out_ln_g,
           L1_attn_out_ln_b, L1_ffn_gate_w, L1_ffn_w0, L1_ffn_b0, L1_ffn_w1, L1_ffn_b1):
    x = f32c(L0_attn_x)
    ident = np.eye(128, dtype=np.float32)
    identb = np.eye(128, dtype=np.float32).astype(BF)
    ones_r = np.ones((1, 128), dtype=np.float32)
    ones_c = np.ones((128, 1), dtype=np.float32)

    maps1 = []
    for c in range(NCORE):
        ck = slice(128 * c, 128 * (c + 1))
        maps1.append({
            "xs": f32c(x[:, :, ck]),
            "a_s": f32c(L0_attn_a[ck]).reshape(128, 1),
            "b_s": f32c(L0_attn_b[ck]).reshape(128, 1),
            "idf": ident,
        })
    r1 = run(build_scan, maps1)
    yT_full = np.stack([np.concatenate([r1[c]["yT"][b] for c in range(NCORE)], axis=0)
                        for b in range(B)], axis=0)

    shared2 = {
        "w_outT": f32c(L0_attn_w_out.T),
        "b_row": f32c(L0_attn_b_out).reshape(1, D),
        "w_fcT": f32c(L0_ffn_w_fc.T),
        "w_projT": f32c(L0_ffn_w_proj.T),
        "uT": f32c(L0_ffn_u.T),
        "vT": f32c(L0_ffn_v.T),
        "fbT": f32c(L0_ffn_fast_b.T),
        "gate": f32c(L0_ffn_gate).reshape(1, 1),
        "onesr": ones_r, "onesc": ones_c, "idf": ident,
    }
    maps2 = []
    for c in range(NCORE):
        b = c // 4
        t0 = 2048 * (c % 4)
        m = dict(shared2)
        m["xs"] = f32c(x[b, t0:t0 + TPC])
        m["yTs"] = f32c(yT_full[b][:, t0:t0 + TPC])
        maps2.append(m)
    r2 = run(build_l2, maps2)
    x2 = np.concatenate([r2[c]["x2o"] for c in range(NCORE)], axis=0).reshape(B, T, D)

    shared3 = {
        "g_in": f32c(L1_attn_in_ln_g).reshape(D, 1),
        "b_in": f32c(L1_attn_in_ln_b).reshape(D, 1),
        "winT": f32c(L1_attn_W_in.T),
        "mg": f32c(L1_attn_mid_ln_g).reshape(1, 7168),
        "mb": f32c(L1_attn_mid_ln_b).reshape(1, 7168),
        "onesr": ones_r, "idf": ident,
    }
    maps3 = []
    for c in range(NCORE):
        b = c // 4
        t0 = 2048 * (c % 4)
        m = dict(shared3)
        if t0 == 0:
            xh = np.concatenate([x2[b, 0:128], x2[b, 0:TPC]], axis=0)
            hcol = np.zeros((128, 1), dtype=np.float32)
        else:
            xh = x2[b, t0 - 128:t0 + TPC]
            hcol = np.ones((128, 1), dtype=np.float32)
        m["x2h"] = f32c(xh)
        m["halo"] = hcol
        maps3.append(m)
    r3 = run(build_l3, maps3)
    qkv = np.concatenate([r3[c]["qkvo"] for c in range(NCORE)], axis=0).reshape(B, T, 3072)
    pg = np.concatenate([r3[c]["pgo"] for c in range(NCORE)], axis=0).reshape(B, T, 4096)

    r_idx = np.arange(128)
    c_idx = np.arange(512)
    caus = np.concatenate(
        [np.where(r_idx[:, None] > c_idx[None, :] - 128 * v, np.float32(-1e10), np.float32(0.0))
         for v in range(4)], axis=1).astype(np.float32)
    iota2d = (r_idx[:, None] + 128 * np.arange(64)[None, :]).astype(np.float32)
    slopes = f32c(L1_attn_slopes)
    maps4 = []
    for c in range(NCORE):
        prs = [(pp // 16, pp % 16) for pp in range(4 * c, 4 * c + 4)]
        maps4.append({
            "q_in": bfc(np.stack([qkv[b, :, 64 * h:64 * h + 64] for (b, h) in prs])),
            "k_in": bfc(np.stack([qkv[b, :, 1024 + 64 * h:1024 + 64 * h + 64] for (b, h) in prs])),
            "v_in": bfc(np.stack([qkv[b, :, 2048 + 64 * h:2048 + 64 * h + 64] for (b, h) in prs])),
            "sl_in": np.stack([slopes[h:h + 1] for (b, h) in prs]),
            "iota": iota2d, "caus": caus, "idb": identb, "onesr": ones_r,
        })
    r4 = run(build_l4, maps4)
    oT_full = np.zeros((B, D, T), dtype=BF)
    for c in range(NCORE):
        for i, pp in enumerate(range(4 * c, 4 * c + 4)):
            b, h = pp // 16, pp % 16
            oT_full[b, 64 * h:64 * h + 64, :] = r4[c]["oT"][i]

    shared5 = {
        "woT": f32c(L1_attn_W_out.T),
        "og": f32c(L1_attn_out_ln_g).reshape(1, D),
        "ob": f32c(L1_attn_out_ln_b).reshape(1, D),
        "gw": f32c(L1_ffn_gate_w),
        "w0T": f32c(L1_ffn_w0.T),
        "b0r": f32c(L1_ffn_b0).reshape(1, D),
        "w1T": f32c(L1_ffn_w1.T),
        "b1r": f32c(L1_ffn_b1).reshape(1, D),
        "onesr": ones_r, "idf": ident, "idb": identb,
    }
    maps5 = []
    for c in range(NCORE):
        b = c // 4
        t0 = 2048 * (c % 4)
        m = dict(shared5)
        m["oTs"] = bfc(oT_full[b][:, t0:t0 + TPC])
        m["pgs"] = bfc(pg[b, t0:t0 + TPC])
        m["x2s"] = f32c(x2[b, t0:t0 + TPC])
        maps5.append(m)
    r5 = run(build_l5, maps5)
    out = np.concatenate([r5[c]["yo"] for c in range(NCORE)], axis=0).reshape(B, T, D)
    return out.astype(np.float32)


# version 03
# speedup vs baseline: 1.1373x; 1.1373x over previous
import numpy as np
import ml_dtypes
import concourse.bass as bass
import concourse.mybir as mybir
from concourse.bass_utils import run_bass_kernel_spmd
from contextlib import ExitStack

F32 = mybir.dt.float32
BF16 = mybir.dt.bfloat16
AF = mybir.ActivationFunctionType
OP = mybir.AluOpType
AX = mybir.AxisListType.X
BF = ml_dtypes.bfloat16

NCORE = 8
B, T, D = 2, 8192, 1024
TPC = (B * T) // NCORE


class Prog:

    def __init__(self):
        self.items = []

    def t(self, fn):
        self.items.append(("tensor", fn, 1))

    def v(self, fn):
        self.items.append(("vector", fn, 1))

    def s(self, fn):
        self.items.append(("scalar", fn, 1))

    def dma(self, out, in_):
        self.items.append(("sync", lambda e, o=out, i=in_: e.dma_start(out=o, in_=i), 16))

    def emit(self, nc, S, block):
        items = self.items
        cum = [0]
        for (_, _, inc) in items:
            cum.append(cum[-1] + inc)
        total = cum[-1]
        relaxed = {"tensor", "sync"}
        wtgt = [0] * len(items)
        run0 = 0
        for i, (en, _, _) in enumerate(items):
            if i > 0 and en != items[i - 1][0]:
                run0 = i
            wtgt[i] = cum[run0] if en in relaxed else cum[i]
        engines = ["sync", "tensor", "vector", "scalar"]

        def mk(name):
            def f(eng):
                last = -1
                for i, (en, fn, inc) in enumerate(items):
                    if en != name:
                        continue
                    if wtgt[i] > last:
                        eng.wait_ge(S, wtgt[i])
                        last = wtgt[i]
                    fn(eng).then_inc(S, inc)
                if name == "sync":
                    eng.wait_ge(S, total)
            return f

        for en in engines:
            getattr(block, en)(mk(en))


def ln_plain(p, buf, W, sumc, negm, stdc, sqscr, accc, epsc):
    nch = (W + 511) // 512
    p.v(lambda e, b=buf: e.reduce_sum(out=sumc[:], in_=b, axis=AX))
    p.v(lambda e: e.tensor_scalar_mul(out=negm[:], in0=sumc[:], scalar1=-1.0 / W))
    p.s(lambda e, b=buf: e.activation(out=b, in_=b, func=AF.Identity, bias=negm[:]))
    for c in range(nch):
        w0 = 512 * c
        w1 = min(W, w0 + 512)
        p.s(lambda e, b=buf, w0=w0, w1=w1, c=c: e.activation(
            out=sqscr[:, 0:w1 - w0], in_=b[:, w0:w1], func=AF.Square,
            accum_out=accc[:, c:c + 1]))
    p.v(lambda e, n=nch: e.reduce_sum(out=sumc[:], in_=accc[:, 0:n], axis=AX))
    p.s(lambda e: e.activation(out=stdc[:], in_=sumc[:], func=AF.Sqrt, scale=1.0 / W, bias=epsc[:]))
    p.v(lambda e: e.reciprocal(out=stdc[:], in_=stdc[:]))
    p.v(lambda e, b=buf: e.tensor_scalar_mul(out=b, in0=b, scalar1=stdc[:]))


def run(build_fn, in_maps):
    nc = bass.Bass("TRN2", target_bir_lowering=False)
    build_fn(nc)
    res = run_bass_kernel_spmd(nc, in_maps, core_ids=list(range(NCORE)))
    return res.results


def f32c(a):
    return np.ascontiguousarray(a, dtype=np.float32)


def bfc(a):
    return np.ascontiguousarray(a)


def build_scan(nc):
    xs = nc.dram_tensor("xs", [B, T, 128], F32, kind="ExternalInput").ap()
    a_s = nc.dram_tensor("a_s", [128, 1], F32, kind="ExternalInput").ap()
    b_s = nc.dram_tensor("b_s", [128, 1], F32, kind="ExternalInput").ap()
    idf = nc.dram_tensor("idf", [128, 128], F32, kind="ExternalInput").ap()
    yT = nc.dram_tensor("yT", [B, 128, T], F32, kind="ExternalOutput").ap()
    with ExitStack() as st:
        xT = st.enter_context(nc.sbuf_tensor("xT", [128, T], F32))
        bx = st.enter_context(nc.sbuf_tensor("bx", [128, T], F32))
        abc = st.enter_context(nc.sbuf_tensor("abc", [128, T], F32))
        acol = st.enter_context(nc.sbuf_tensor("acol", [128, 1], F32))
        bcol = st.enter_context(nc.sbuf_tensor("bcol", [128, 1], F32))
        ident = st.enter_context(nc.sbuf_tensor("ident", [128, 128], F32))
        tin = st.enter_context(nc.sbuf_tensor("tin", [128, 128], F32))
        tp = st.enter_context(nc.psum_tensor("tp", [128, 128], F32))
        S = st.enter_context(nc.semaphore("S"))
        block = st.enter_context(nc.Block())
        p = Prog()
        p.dma(ident[:], idf[:, :])
        p.dma(acol[:], a_s[:, :])
        p.dma(bcol[:], b_s[:, :])
        p.s(lambda e: e.activation(out=acol[:], in_=acol[:], func=AF.Tanh))
        p.v(lambda e: e.memset(abc[:], 1.0))
        p.v(lambda e: e.tensor_scalar_mul(out=abc[:], in0=abc[:], scalar1=acol[:]))
        for b in range(B):
            for j in range(T // 128):
                p.dma(tin[:], xs[b, 128 * j:128 * (j + 1), :])
                p.t(lambda e: e.transpose(out=tp[:], in_=tin[:], identity=ident[:]))
                p.v(lambda e, j=j: e.tensor_copy(out=xT[:, 128 * j:128 * (j + 1)], in_=tp[:]))
            p.v(lambda e: e.tensor_scalar_mul(out=bx[:], in0=xT[:], scalar1=bcol[:]))
            for c in range(4):
                c0 = 2048 * c
                if c == 0:
                    p.v(lambda e: e.tensor_tensor_scan(
                        out=xT[:, 0:2048], data0=abc[:, 0:2048], data1=bx[:, 0:2048],
                        initial=0.0, op0=OP.mult, op1=OP.add))
                else:
                    p.v(lambda e, c0=c0: e.tensor_tensor_scan(
                        out=xT[:, c0:c0 + 2048], data0=abc[:, c0:c0 + 2048],
                        data1=bx[:, c0:c0 + 2048], initial=xT[:, c0 - 1:c0],
                        op0=OP.mult, op1=OP.add))
            p.dma(yT[b, :, :], xT[:])
        p.emit(nc, S, block)


def build_l2(nc):
    xs = nc.dram_tensor("xs", [TPC, D], F32, kind="ExternalInput").ap()
    yTs = nc.dram_tensor("yTs", [D, TPC], F32, kind="ExternalInput").ap()
    w_outT = nc.dram_tensor("w_outT", [D, D], F32, kind="ExternalInput").ap()
    b_row = nc.dram_tensor("b_row", [1, D], F32, kind="ExternalInput").ap()
    w_fcT = nc.dram_tensor("w_fcT", [D, 4096], F32, kind="ExternalInput").ap()
    w_projT = nc.dram_tensor("w_projT", [4096, D], F32, kind="ExternalInput").ap()
    uT = nc.dram_tensor("uT", [16, 4096], F32, kind="ExternalInput").ap()
    vT = nc.dram_tensor("vT", [D, 16], F32, kind="ExternalInput").ap()
    fbT = nc.dram_tensor("fbT", [16, 16], F32, kind="ExternalInput").ap()
    gate = nc.dram_tensor("gate", [1, 1], F32, kind="ExternalInput").ap()
    onesr = nc.dram_tensor("onesr", [1, 128], F32, kind="ExternalInput").ap()
    onesc = nc.dram_tensor("onesc", [128, 1], F32, kind="ExternalInput").ap()
    idf = nc.dram_tensor("idf", [128, 128], F32, kind="ExternalInput").ap()
    w2hbm = nc.dram_tensor("w2hbm", [32, 128, D], BF16, kind="Internal").ap()
    x2o = nc.dram_tensor("x2o", [TPC, D], F32, kind="ExternalOutput").ap()
    with ExitStack() as st:
        sb = lambda n, sh, dt=F32: st.enter_context(nc.sbuf_tensor(n, sh, dt))
        w1e = sb("w1e", [128, 8 * 4096], BF16)
        scbc = sb("scbc", [128, 4096], F32)
        scbc2 = sb("scbc2", [128, 1024], F32)
        shalf = sb("shalf", [128, 1024], F32)
        scrow = sb("scrow", [1, 4096], F32)
        wbuf = sb("wbuf", [128, 1024], F32)
        zb = sb("zb", [128, 1024], F32)
        t1b = sb("t1b", [128, 1024], F32)
        stg = sb("stg", [128, 1024], BF16)
        wo_bf = sb("wo_bf", [128, 8 * 1024], BF16)
        vt_bf = sb("vt_bf", [128, 128], BF16)
        ut_bf = sb("ut_bf", [16, 4096], BF16)
        fb_bf = sb("fb_bf", [16, 16], BF16)
        g16 = sb("g16", [16, 1], F32)
        brow = sb("brow", [1, 1024], F32)
        o1r = sb("o1r", [1, 128], F32)
        o1c = sb("o1c", [128, 1], F32)
        ident = sb("ident", [128, 128], F32)
        gsb = sb("gsb", [1, 1], F32)
        X = sb("X", [128, 2048], F32)
        X1 = sb("X1", [128, 2048], F32)
        ytf = sb("ytf", [128, 2048], F32)
        yt_bf = sb("yt_bf", [128, 2048], BF16)
        x1T = sb("x1T", [128, 2048], BF16)
        rsq = sb("rsq", [128, 8192], BF16)
        rl = sb("rl", [128, 256], F32)
        t1sb = sb("t1sb", [16, 256], BF16)
        t2s = sb("t2s", [16, 256], BF16)
        w2buf = sb("w2buf", [128, 1024], BF16)
        sumc = sb("sumc", [128, 1], F32)
        negm = sb("negm", [128, 1], F32)
        stdc = sb("stdc", [128, 1], F32)
        sqscr = sb("sqscr", [128, 512], F32)
        accc = sb("accc", [128, 4], F32)
        epsc = sb("epsc", [128, 1], F32)
        ps = lambda n, sh, dt=F32: st.enter_context(nc.psum_tensor(n, sh, dt))
        P0 = ps("P0", [128, 512]); P1 = ps("P1", [128, 512])
        P2 = ps("P2", [128, 512]); P3 = ps("P3", [128, 512])
        Pt = ps("Pt", [128, 128]); Plr = ps("Plr", [16, 256])
        P2lr = ps("P2lr", [16, 256]); Ph = ps("Ph", [128, 512])
        POUT = [P0, P1, P2, P3]
        S = st.enter_context(nc.semaphore("S"))
        block = st.enter_context(nc.Block())
        p = Prog()
        p.v(lambda e: e.memset(epsc[:], 1e-5))
        p.dma(ident[:], idf[:, :])
        p.dma(o1r[:], onesr[:, :])
        p.dma(o1c[:], onesc[:, :])
        p.dma(brow[:], b_row[:, :])
        p.dma(gsb[:], gate[:, :])
        p.t(lambda e: e.matmul(out=Plr[0:16, 0:1], lhsT=o1r[0:1, 0:16], rhs=gsb[0:1, 0:1],
                               start=True, stop=True, skip_group_check=True))
        p.v(lambda e: e.tensor_copy(out=g16[:], in_=Plr[0:16, 0:1]))
        for c8 in range(8):
            for k in range(8):
                p.dma(wbuf[:, 0:512], w_fcT[128 * k:128 * (k + 1), 512 * c8:512 * (c8 + 1)])
                p.s(lambda e: e.activation(out=zb[:, 0:512], in_=wbuf[:, 0:512], func=AF.Abs))
                p.t(lambda e, k=k: e.matmul(out=P0[0:1, :], lhsT=o1c[:, 0:1], rhs=zb[:, 0:512],
                                            start=(k == 0), stop=(k == 7), skip_group_check=True))
            p.s(lambda e, c8=c8: e.activation(out=scrow[0:1, 512 * c8:512 * (c8 + 1)],
                                              in_=P0[0:1, :], func=AF.Identity, scale=1.0 / 1024))
        p.v(lambda e: e.tensor_scalar_max(out=scrow[:], in0=scrow[:], scalar1=1e-5))
        for c8 in range(8):
            p.t(lambda e, c8=c8: e.matmul(out=P0[:], lhsT=o1r[0:1, :],
                                          rhs=scrow[0:1, 512 * c8:512 * (c8 + 1)],
                                          start=True, stop=True, skip_group_check=True))
            p.v(lambda e, c8=c8: e.tensor_copy(out=scbc[:, 512 * c8:512 * (c8 + 1)], in_=P0[:]))
        for k in range(8):
            for c in range(4):
                cc = 1024 * c
                p.dma(wbuf[:], w_fcT[128 * k:128 * (k + 1), cc:cc + 1024])
                p.v(lambda e, cc=cc: e.tensor_scalar_mul(out=shalf[:], in0=scbc[:, cc:cc + 1024],
                                                         scalar1=0.5))
                p.s(lambda e: e.activation(out=zb[:], in_=wbuf[:], func=AF.Abs))
                p.v(lambda e: e.tensor_tensor(out=t1b[:], in0=zb[:], in1=shalf[:], op=OP.is_ge))
                p.s(lambda e: e.activation(out=zb[:], in_=wbuf[:], func=AF.Sign))
                p.v(lambda e: e.tensor_tensor(out=t1b[:], in0=t1b[:], in1=zb[:], op=OP.mult))
                p.v(lambda e, k=k, cc=cc: e.tensor_tensor(
                    out=w1e[:, 4096 * k + cc:4096 * k + cc + 1024], in0=t1b[:],
                    in1=scbc[:, cc:cc + 1024], op=OP.mult))
        for c2 in range(2):
            for k2 in range(32):
                p.dma(wbuf[:, 0:512], w_projT[128 * k2:128 * (k2 + 1), 512 * c2:512 * (c2 + 1)])
                p.s(lambda e: e.activation(out=zb[:, 0:512], in_=wbuf[:, 0:512], func=AF.Abs))
                p.t(lambda e, k2=k2: e.matmul(out=P0[0:1, :], lhsT=o1c[:, 0:1], rhs=zb[:, 0:512],
                                              start=(k2 == 0), stop=(k2 == 31), skip_group_check=True))
            p.s(lambda e, c2=c2: e.activation(out=scrow[0:1, 512 * c2:512 * (c2 + 1)],
                                              in_=P0[0:1, :], func=AF.Identity, scale=1.0 / 4096))
        p.v(lambda e: e.tensor_scalar_max(out=scrow[:, 0:1024], in0=scrow[:, 0:1024], scalar1=1e-5))
        for c2 in range(2):
            p.t(lambda e, c2=c2: e.matmul(out=P0[:], lhsT=o1r[0:1, :],
                                          rhs=scrow[0:1, 512 * c2:512 * (c2 + 1)],
                                          start=True, stop=True, skip_group_check=True))
            p.v(lambda e, c2=c2: e.tensor_copy(out=scbc2[:, 512 * c2:512 * (c2 + 1)], in_=P0[:]))
        p.v(lambda e: e.tensor_scalar_mul(out=shalf[:], in0=scbc2[:], scalar1=0.5))
        for k2 in range(32):
            p.dma(wbuf[:], w_projT[128 * k2:128 * (k2 + 1), :])
            p.s(lambda e: e.activation(out=zb[:], in_=wbuf[:], func=AF.Abs))
            p.v(lambda e: e.tensor_tensor(out=t1b[:], in0=zb[:], in1=shalf[:], op=OP.is_ge))
            p.s(lambda e: e.activation(out=zb[:], in_=wbuf[:], func=AF.Sign))
            p.v(lambda e: e.tensor_tensor(out=t1b[:], in0=t1b[:], in1=zb[:], op=OP.mult))
            p.v(lambda e: e.tensor_tensor(out=stg[:], in0=t1b[:], in1=scbc2[:], op=OP.mult))
            p.dma(w2hbm[k2, :, :], stg[:])
        for k in range(8):
            p.dma(wbuf[:], w_outT[128 * k:128 * (k + 1), :])
            p.v(lambda e, k=k: e.tensor_copy(out=wo_bf[:, 1024 * k:1024 * (k + 1)], in_=wbuf[:]))
        for k in range(8):
            p.dma(wbuf[:, 0:16], vT[128 * k:128 * (k + 1), :])
            p.v(lambda e, k=k: e.tensor_copy(out=vt_bf[:, 16 * k:16 * (k + 1)], in_=wbuf[:, 0:16]))
        for c in range(4):
            p.dma(wbuf[0:16, :], uT[:, 1024 * c:1024 * (c + 1)])
            p.v(lambda e, c=c: e.tensor_copy(out=ut_bf[:, 1024 * c:1024 * (c + 1)],
                                             in_=wbuf[0:16, :]))
        p.dma(wbuf[0:16, 0:16], fbT[:, :])
        p.v(lambda e: e.tensor_copy(out=fb_bf[:], in_=wbuf[0:16, 0:16]))
        for s8 in range(8):
            t0 = 256 * s8
            for tl in range(2):
                p.dma(X[:, 1024 * tl:1024 * (tl + 1)], xs[t0 + 128 * tl:t0 + 128 * (tl + 1), :])
            for k in range(8):
                p.dma(ytf[:, 256 * k:256 * (k + 1)], yTs[128 * k:128 * (k + 1), t0:t0 + 256])
            p.v(lambda e: e.tensor_copy(out=yt_bf[:], in_=ytf[:]))
            for tl in range(2):
                for E in range(2):
                    PM = POUT[2 * tl + E]
                    for k in range(8):
                        p.t(lambda e, PM=PM, k=k, tl=tl, E=E: e.matmul(
                            out=PM[:], lhsT=yt_bf[:, 256 * k + 128 * tl:256 * k + 128 * tl + 128],
                            rhs=wo_bf[:, 1024 * k + 512 * E:1024 * k + 512 * E + 512],
                            start=(k == 0), stop=False, skip_group_check=True))
                    p.t(lambda e, PM=PM, E=E: e.matmul(
                        out=PM[:], lhsT=o1r[0:1, :], rhs=brow[0:1, 512 * E:512 * E + 512],
                        start=False, stop=True, skip_group_check=True))
                    p.v(lambda e, PM=PM, tl=tl, E=E: e.tensor_tensor(
                        out=X1[:, 1024 * tl + 512 * E:1024 * tl + 512 * E + 512],
                        in0=X[:, 1024 * tl + 512 * E:1024 * tl + 512 * E + 512],
                        in1=PM[:], op=OP.add))
                ln_plain(p, X1[:, 1024 * tl:1024 * (tl + 1)], 1024, sumc, negm, stdc, sqscr, accc, epsc)
                for k in range(8):
                    p.t(lambda e, tl=tl, k=k: e.transpose(
                        out=Pt[:], in_=X1[:, 1024 * tl + 128 * k:1024 * tl + 128 * k + 128],
                        identity=ident[:]))
                    p.v(lambda e, tl=tl, k=k: e.tensor_copy(
                        out=x1T[:, 256 * k + 128 * tl:256 * k + 128 * tl + 128], in_=Pt[:]))
            for k in range(8):
                p.t(lambda e, k=k: e.matmul(out=Plr[:], lhsT=vt_bf[:, 16 * k:16 * (k + 1)],
                                            rhs=x1T[:, 256 * k:256 * (k + 1)],
                                            start=(k == 0), stop=(k == 7), skip_group_check=True))
            p.v(lambda e: e.tensor_copy(out=t1sb[:], in_=Plr[:]))
            p.t(lambda e: e.matmul(out=P2lr[:], lhsT=fb_bf[:], rhs=t1sb[:],
                                   start=True, stop=True, skip_group_check=True))
            p.v(lambda e: e.tensor_scalar_mul(out=t2s[:], in0=P2lr[:], scalar1=g16[:]))
            for H in range(32):
                for k in range(8):
                    p.t(lambda e, H=H, k=k: e.matmul(
                        out=Ph[:, 0:256], lhsT=w1e[:, 4096 * k + 128 * H:4096 * k + 128 * H + 128],
                        rhs=x1T[:, 256 * k:256 * (k + 1)],
                        start=(k == 0), stop=False, skip_group_check=True))
                p.t(lambda e, H=H: e.matmul(out=Ph[:, 0:256], lhsT=ut_bf[0:16, 128 * H:128 * H + 128],
                                            rhs=t2s[:], start=False, stop=True, skip_group_check=True))
                p.s(lambda e: e.activation(out=rl[:], in_=Ph[:, 0:256], func=AF.Relu))
                p.s(lambda e, H=H: e.activation(out=rsq[:, 256 * H:256 * (H + 1)], in_=rl[:],
                                                func=AF.Square))
            for k2 in range(32):
                p.dma(w2buf[:], w2hbm[k2, :, :])
                for tl in range(2):
                    for E in range(2):
                        p.t(lambda e, k2=k2, tl=tl, E=E: e.matmul(
                            out=POUT[2 * tl + E][:],
                            lhsT=rsq[:, 256 * k2 + 128 * tl:256 * k2 + 128 * tl + 128],
                            rhs=w2buf[:, 512 * E:512 * E + 512],
                            start=(k2 == 0), stop=(k2 == 31), skip_group_check=True))
            for tl in range(2):
                for E in range(2):
                    p.v(lambda e, tl=tl, E=E: e.tensor_tensor(
                        out=X[:, 1024 * tl + 512 * E:1024 * tl + 512 * E + 512],
                        in0=X1[:, 1024 * tl + 512 * E:1024 * tl + 512 * E + 512],
                        in1=POUT[2 * tl + E][:], op=OP.add))
                ln_plain(p, X[:, 1024 * tl:1024 * (tl + 1)], 1024, sumc, negm, stdc, sqscr, accc, epsc)
                p.dma(x2o[t0 + 128 * tl:t0 + 128 * (tl + 1), :], X[:, 1024 * tl:1024 * (tl + 1)])
        p.emit(nc, S, block)


def build_l3(nc):
    QKVP, HID = 7168, 1024
    x2h = nc.dram_tensor("x2h", [TPC + 128, D], F32, kind="ExternalInput").ap()
    halo = nc.dram_tensor("halo", [128, 1], F32, kind="ExternalInput").ap()
    g_in = nc.dram_tensor("g_in", [D, 1], F32, kind="ExternalInput").ap()
    b_in = nc.dram_tensor("b_in", [D, 1], F32, kind="ExternalInput").ap()
    winT = nc.dram_tensor("winT", [D, QKVP], F32, kind="ExternalInput").ap()
    mg = nc.dram_tensor("mg", [1, QKVP], F32, kind="ExternalInput").ap()
    mb = nc.dram_tensor("mb", [1, QKVP], F32, kind="ExternalInput").ap()
    onesr = nc.dram_tensor("onesr", [1, 128], F32, kind="ExternalInput").ap()
    idf = nc.dram_tensor("idf", [128, 128], F32, kind="ExternalInput").ap()
    qkvo = nc.dram_tensor("qkvo", [TPC, 3072], BF16, kind="ExternalOutput").ap()
    pgo = nc.dram_tensor("pgo", [TPC, 4096], BF16, kind="ExternalOutput").ap()
    with ExitStack() as st:
        sb = lambda n, sh, dt=F32: st.enter_context(nc.sbuf_tensor(n, sh, dt))
        win_bf = sb("win_bf", [128, 8 * QKVP], BF16)
        mgbc = sb("mgbc", [128, QKVP], BF16)
        mbbc = sb("mbbc", [128, QKVP], BF16)
        h_sb = sb("h_sb", [128, QKVP], F32)
        gcols = sb("gcols", [128, 8], F32)
        bcols = sb("bcols", [128, 8], F32)
        bhcols = sb("bhcols", [128, 8], F32)
        halocol = sb("halocol", [128, 1], F32)
        xt = sb("xt", [128, 1024], F32)
        xbuf = sb("xbuf", [128, 2048], BF16)
        egc = sb("egc", [128, 512], F32)
        pgb = sb("pgb", [128, 4096], BF16)
        o1r = sb("o1r", [1, 128], F32)
        ident = sb("ident", [128, 128], F32)
        sumc = sb("sumc", [128, 1], F32)
        negm = sb("negm", [128, 1], F32)
        stdc = sb("stdc", [128, 1], F32)
        sqscr = sb("sqscr", [128, 512], F32)
        accc = sb("accc", [128, 16], F32)
        epsc = sb("epsc", [128, 1], F32)
        Pm = st.enter_context(nc.psum_tensor("Pm", [128, 512], F32))
        Pt = st.enter_context(nc.psum_tensor("Pt", [128, 128], F32))
        S = st.enter_context(nc.semaphore("S"))
        block = st.enter_context(nc.Block())
        p = Prog()
        p.v(lambda e: e.memset(epsc[:], 1e-5))
        p.dma(ident[:], idf[:, :])
        p.dma(o1r[:], onesr[:, :])
        p.dma(halocol[:], halo[:, :])
        for k in range(8):
            p.dma(gcols[:, k:k + 1], g_in[128 * k:128 * (k + 1), :])
            p.dma(bcols[:, k:k + 1], b_in[128 * k:128 * (k + 1), :])
        p.v(lambda e: e.tensor_scalar_mul(out=bhcols[:], in0=bcols[:], scalar1=halocol[:]))
        for src, dst in ((mg, mgbc), (mb, mbbc)):
            p.dma(h_sb[0:1, :], src[:, :])
            for c in range(14):
                p.t(lambda e, c=c: e.matmul(out=Pm[:], lhsT=o1r[0:1, :],
                                            rhs=h_sb[0:1, 512 * c:512 * (c + 1)],
                                            start=True, stop=True, skip_group_check=True))
                p.v(lambda e, dst=dst, c=c: e.tensor_copy(out=dst[:, 512 * c:512 * (c + 1)], in_=Pm[:]))
        for k in range(8):
            p.dma(h_sb[:], winT[128 * k:128 * (k + 1), :])
            p.v(lambda e, k=k: e.tensor_copy(out=win_bf[:, QKVP * k:QKVP * (k + 1)], in_=h_sb[:]))
        for tau in range(-1, 16):
            r0 = 128 * (tau + 1)
            p.dma(xt[:], x2h[r0:r0 + 128, :])
            if tau == -1:
                p.v(lambda e: e.tensor_scalar_mul(out=xt[:], in0=xt[:], scalar1=halocol[:]))
            ln_plain(p, xt[:], 1024, sumc, negm, stdc, sqscr, accc, epsc)
            if tau >= 0:
                for k in range(8):
                    p.v(lambda e, k=k: e.tensor_copy(out=xbuf[:, 256 * k:256 * k + 128],
                                                     in_=xbuf[:, 256 * k + 128:256 * k + 256]))
            bsel = bhcols if tau == -1 else bcols
            for k in range(8):
                p.t(lambda e, k=k: e.transpose(out=Pt[:], in_=xt[:, 128 * k:128 * (k + 1)],
                                               identity=ident[:]))
                p.v(lambda e, k=k, bsel=bsel: e.tensor_scalar(
                    out=xbuf[:, 256 * k + 128:256 * k + 256], in0=Pt[:],
                    scalar1=gcols[:, k:k + 1], scalar2=bsel[:, k:k + 1],
                    op0=OP.mult, op1=OP.add))
            if tau < 0:
                continue
            shifts = [1, 3, 0, 0, 0, 0, 0, 0]
            for O in range(14):
                for k in range(8):
                    off = 256 * k + 128 - shifts[k]
                    p.t(lambda e, O=O, k=k, off=off: e.matmul(
                        out=Pm[:], lhsT=xbuf[:, off:off + 128],
                        rhs=win_bf[:, QKVP * k + 512 * O:QKVP * k + 512 * O + 512],
                        start=(k == 0), stop=(k == 7), skip_group_check=True))
                p.s(lambda e, O=O: e.activation(out=h_sb[:, 512 * O:512 * (O + 1)], in_=Pm[:],
                                                func=AF.Relu, accum_out=accc[:, O:O + 1]))
            p.v(lambda e: e.reduce_sum(out=sumc[:], in_=accc[:, 0:14], axis=AX))
            p.v(lambda e: e.tensor_scalar_mul(out=negm[:], in0=sumc[:], scalar1=-1.0 / QKVP))
            p.s(lambda e: e.activation(out=h_sb[:], in_=h_sb[:], func=AF.Identity, bias=negm[:]))
            for c in range(14):
                p.s(lambda e, c=c: e.activation(out=sqscr[:], in_=h_sb[:, 512 * c:512 * (c + 1)],
                                                func=AF.Square, accum_out=accc[:, c:c + 1]))
            p.v(lambda e: e.reduce_sum(out=sumc[:], in_=accc[:, 0:14], axis=AX))
            p.s(lambda e: e.activation(out=stdc[:], in_=sumc[:], func=AF.Sqrt,
                                       scale=1.0 / QKVP, bias=epsc[:]))
            p.v(lambda e: e.reciprocal(out=stdc[:], in_=stdc[:]))
            p.v(lambda e: e.tensor_scalar_mul(out=h_sb[:], in0=h_sb[:], scalar1=stdc[:]))
            p.v(lambda e: e.tensor_tensor(out=h_sb[:], in0=h_sb[:], in1=mgbc[:], op=OP.mult))
            p.v(lambda e: e.tensor_tensor(out=h_sb[:], in0=h_sb[:], in1=mbbc[:], op=OP.add))
            p.v(lambda e: e.tensor_copy(out=pgb[:, 0:3072], in_=h_sb[:, 0:3072]))
            p.dma(qkvo[128 * tau:128 * (tau + 1), :], pgb[:, 0:3072])
            for c in range(8):
                pc = 3072 + 512 * c
                p.s(lambda e, pc=pc: e.activation(out=egc[:], in_=h_sb[:, pc:pc + 512],
                                                  func=AF.Erf, scale=0.7071067811865476))
                p.v(lambda e: e.tensor_scalar(out=egc[:], in0=egc[:], scalar1=0.5, scalar2=0.5,
                                              op0=OP.mult, op1=OP.add))
                p.v(lambda e, pc=pc: e.tensor_tensor(out=egc[:], in0=egc[:],
                                                     in1=h_sb[:, pc:pc + 512], op=OP.mult))
                if c == 0:
                    p.v(lambda e: e.tensor_tensor(out=pgb[:, 1:512], in0=egc[:, 1:512],
                                                  in1=h_sb[:, 3072:3072 + 511], op=OP.mult))
                    p.v(lambda e: e.tensor_tensor(out=pgb[:, 0:1], in0=egc[:, 0:1],
                                                  in1=h_sb[:, 7167:7168], op=OP.mult))
                else:
                    p.v(lambda e, c=c, pc=pc: e.tensor_tensor(
                        out=pgb[:, 512 * c:512 * (c + 1)], in0=egc[:],
                        in1=h_sb[:, pc - 1:pc + 511], op=OP.mult))
            p.dma(pgo[128 * tau:128 * (tau + 1), :], pgb[:])
        p.emit(nc, S, block)


def build_l4(nc):
    L = T
    q_in = nc.dram_tensor("q_in", [4, L, 64], BF16, kind="ExternalInput").ap()
    k_in = nc.dram_tensor("k_in", [4, L, 64], BF16, kind="ExternalInput").ap()
    v_in = nc.dram_tensor("v_in", [4, L, 64], BF16, kind="ExternalInput").ap()
    sl_in = nc.dram_tensor("sl_in", [4, 1], F32, kind="ExternalInput").ap()
    iota = nc.dram_tensor("iota", [128, 64], F32, kind="ExternalInput").ap()
    caus = nc.dram_tensor("caus", [128, 2048], F32, kind="ExternalInput").ap()
    idb = nc.dram_tensor("idb", [128, 128], BF16, kind="ExternalInput").ap()
    onesr = nc.dram_tensor("onesr", [1, 128], F32, kind="ExternalInput").ap()
    oT = nc.dram_tensor("oT", [4, 64, L], BF16, kind="ExternalOutput").ap()
    with ExitStack() as st:
        sb = lambda n, sh, dt=F32: st.enter_context(nc.sbuf_tensor(n, sh, dt))
        qT = sb("qT", [64, L], BF16)
        kT = sb("kT", [64, L], BF16)
        vaug = sb("vaug", [128, 65 * 64], BF16)
        qbuf = sb("qbuf", [128, 4096], BF16)
        ssb = sb("ssb", [128, 512], F32)
        pbf = sb("pbf", [128, 512], BF16)
        Macc = sb("Macc", [128, 512], F32)
        mred = sb("mred", [64, 512], F32)
        negr = sb("negr", [1, 512], F32)
        nbc = sb("nbc", [128, 512], F32)
        alall = sb("alall", [128, 64], F32)
        iot = sb("iot", [128, 64], F32)
        casb = sb("casb", [128, 2048], F32)
        o1r = sb("o1r", [1, 128], F32)
        idsb = sb("idsb", [128, 128], BF16)
        slsb = sb("slsb", [1, 1], F32)
        slcol = sb("slcol", [128, 1], F32)
        of32 = sb("of32", [64, 512], F32)
        obf = sb("obf", [64, 512], BF16)
        rrow = sb("rrow", [1, 512], F32)
        ps_s = st.enter_context(nc.psum_tensor("ps_s", [128, 512], F32))
        ps_o = st.enter_context(nc.psum_tensor("ps_o", [65, 512], F32))
        ps_b = st.enter_context(nc.psum_tensor("ps_b", [128, 512], F32))
        ps_t = st.enter_context(nc.psum_tensor("ps_t", [64, 128], BF16))
        ps_r = st.enter_context(nc.psum_tensor("ps_r", [64, 512], F32))
        ps_c = st.enter_context(nc.psum_tensor("ps_c", [128, 1], F32))
        S = st.enter_context(nc.semaphore("S"))
        block = st.enter_context(nc.Block())
        p = Prog()
        p.dma(iot[:], iota[:, :])
        p.dma(casb[:], caus[:, :])
        p.dma(idsb[:], idb[:, :])
        p.dma(o1r[:], onesr[:, :])
        for pi in range(4):
            p.dma(slsb[:], sl_in[pi:pi + 1, :])
            p.t(lambda e: e.matmul(out=ps_c[:], lhsT=o1r[0:1, :], rhs=slsb[0:1, 0:1],
                                   start=True, stop=True, skip_group_check=True))
            p.v(lambda e: e.tensor_copy(out=slcol[:], in_=ps_c[:]))
            p.v(lambda e: e.tensor_scalar_mul(out=alall[:], in0=iot[:], scalar1=slcol[:]))
            p.v(lambda e: e.memset(vaug[:], 1.0))
            for g in range(8):
                p.dma(vaug[:].rearrange("p (j c) -> p j c", c=65)[:, 8 * g:8 * (g + 1), 0:64],
                      v_in[pi].rearrange("(j p) c -> p j c", p=128)[:, 8 * g:8 * (g + 1), :])
            for src, dstT in ((q_in, qT), (k_in, kT)):
                for g in range(2):
                    p.dma(qbuf[:].rearrange("p (j c) -> p j c", c=64)[:, 0:32, :],
                          src[pi].rearrange("(j p) c -> p j c", p=128)[:, 32 * g:32 * (g + 1), :])
                    for jj in range(32):
                        j = 32 * g + jj
                        p.t(lambda e, jj=jj: e.transpose(out=ps_t[:], in_=qbuf[:, 64 * jj:64 * (jj + 1)],
                                                         identity=idsb[:]))
                        p.v(lambda e, j=j, dstT=dstT: e.tensor_copy(
                            out=dstT[:, 128 * j:128 * (j + 1)], in_=ps_t[:]))
            for Q in range(16):
                q0 = 512 * Q
                nk = 4 * Q + 4
                for j in range(nk):
                    p.t(lambda e, j=j, q0=q0: e.matmul(
                        out=ps_s[:], lhsT=kT[:, 128 * j:128 * (j + 1)], rhs=qT[:, q0:q0 + 512],
                        start=True, stop=True, skip_group_check=True))
                    p.s(lambda e, j=j: e.activation(out=ssb[:], in_=ps_s[:], func=AF.Identity,
                                                    bias=alall[:, j:j + 1], scale=0.125))
                    if j >= 4 * Q:
                        v_ = j - 4 * Q
                        p.v(lambda e, v_=v_: e.tensor_tensor(
                            out=ssb[:], in0=ssb[:], in1=casb[:, 512 * v_:512 * (v_ + 1)], op=OP.add))
                    if j == 0:
                        p.v(lambda e: e.tensor_copy(out=Macc[:], in_=ssb[:]))
                    else:
                        p.v(lambda e: e.tensor_tensor(out=Macc[:], in0=Macc[:], in1=ssb[:], op=OP.max))
                for sh in (64, 32, 16, 8, 4, 2, 1):
                    p.dma(mred[0:sh, :], Macc[sh:2 * sh, :])
                    p.v(lambda e, sh=sh: e.tensor_tensor(out=Macc[0:sh, :], in0=Macc[0:sh, :],
                                                         in1=mred[0:sh, :], op=OP.max))
                p.v(lambda e: e.tensor_scalar_mul(out=negr[:], in0=Macc[0:1, :], scalar1=-1.0))
                p.t(lambda e: e.matmul(out=ps_b[:], lhsT=o1r[0:1, :], rhs=negr[0:1, :],
                                       start=True, stop=True, skip_group_check=True))
                p.v(lambda e: e.tensor_copy(out=nbc[:], in_=ps_b[:]))
                for j in range(nk):
                    p.t(lambda e, j=j, q0=q0: e.matmul(
                        out=ps_s[:], lhsT=kT[:, 128 * j:128 * (j + 1)], rhs=qT[:, q0:q0 + 512],
                        start=True, stop=True, skip_group_check=True))
                    p.s(lambda e, j=j: e.activation(out=ssb[:], in_=ps_s[:], func=AF.Identity,
                                                    bias=alall[:, j:j + 1], scale=0.125))
                    if j >= 4 * Q:
                        v_ = j - 4 * Q
                        p.v(lambda e, v_=v_: e.tensor_tensor(
                            out=ssb[:], in0=ssb[:], in1=casb[:, 512 * v_:512 * (v_ + 1)], op=OP.add))
                    p.v(lambda e: e.tensor_tensor(out=ssb[:], in0=ssb[:], in1=nbc[:], op=OP.add))
                    p.s(lambda e: e.activation(out=pbf[:], in_=ssb[:], func=AF.Exp))
                    p.t(lambda e, j=j, nk=nk: e.matmul(
                        out=ps_o[:], lhsT=vaug[:, 65 * j:65 * (j + 1)], rhs=pbf[:],
                        start=(j == 0), stop=(j == nk - 1), skip_group_check=True))
                p.v(lambda e: e.reciprocal(out=rrow[:], in_=ps_o[64:65, :]))
                p.t(lambda e: e.matmul(out=ps_r[:], lhsT=o1r[0:1, 0:64], rhs=rrow[0:1, :],
                                       start=True, stop=True, skip_group_check=True))
                p.v(lambda e: e.tensor_copy(out=of32[:], in_=ps_o[0:64, :]))
                p.v(lambda e: e.tensor_tensor(out=obf[:], in0=of32[:], in1=ps_r[:], op=OP.mult))
                p.dma(oT[pi, :, q0:q0 + 512], obf[:])
        p.emit(nc, S, block)


def build_l5(nc):
    VP = 5120
    oTs = nc.dram_tensor("oTs", [D, TPC], BF16, kind="ExternalInput").ap()
    pgs = nc.dram_tensor("pgs", [TPC, 4096], BF16, kind="ExternalInput").ap()
    x2s = nc.dram_tensor("x2s", [TPC, D], F32, kind="ExternalInput").ap()
    woT = nc.dram_tensor("woT", [VP, D], F32, kind="ExternalInput").ap()
    og = nc.dram_tensor("og", [1, D], F32, kind="ExternalInput").ap()
    ob = nc.dram_tensor("ob", [1, D], F32, kind="ExternalInput").ap()
    gw = nc.dram_tensor("gw", [D, 8], F32, kind="ExternalInput").ap()
    w0T = nc.dram_tensor("w0T", [D, D], F32, kind="ExternalInput").ap()
    b0r = nc.dram_tensor("b0r", [1, D], F32, kind="ExternalInput").ap()
    w1T = nc.dram_tensor("w1T", [D, D], F32, kind="ExternalInput").ap()
    b1r = nc.dram_tensor("b1r", [1, D], F32, kind="ExternalInput").ap()
    onesr = nc.dram_tensor("onesr", [1, 128], F32, kind="ExternalInput").ap()
    idf = nc.dram_tensor("idf", [128, 128], F32, kind="ExternalInput").ap()
    idb = nc.dram_tensor("idb", [128, 128], BF16, kind="ExternalInput").ap()
    yo = nc.dram_tensor("yo", [TPC, D], F32, kind="ExternalOutput").ap()
    with ExitStack() as st:
        sb = lambda n, sh, dt=F32: st.enter_context(nc.sbuf_tensor(n, sh, dt))
        wo_bf = sb("wo_bf", [128, 40 * 1024], BF16)
        gw_bf = sb("gw_bf", [128, 64], BF16)
        w0_bf = sb("w0_bf", [128, 8192], BF16)
        w1_bf = sb("w1_bf", [128, 8192], BF16)
        ogbc = sb("ogbc", [128, 1024], F32)
        obbc = sb("obbc", [128, 1024], F32)
        b0sb = sb("b0sb", [1, 1024], F32)
        b1sb = sb("b1sb", [1, 1024], F32)
        wbuf = sb("wbuf", [128, 1024], F32)
        o1r = sb("o1r", [1, 128], F32)
        identf = sb("identf", [128, 128], F32)
        identb = sb("identb", [128, 128], BF16)
        xt = sb("xt", [128, 1024], F32)
        x3 = sb("x3", [128, 1024], F32)
        obuf = sb("obuf", [128, 1024], F32)
        ebuf = sb("ebuf", [128, 1024], F32)
        x3T = sb("x3T", [128, 1024], BF16)
        otb = sb("otb", [128, 1024], BF16)
        pgt = sb("pgt", [128, 4096], BF16)
        pgT = sb("pgT", [128, 4096], BF16)
        wsm = sb("wsm", [128, 8], F32)
        gb8 = sb("gb8", [128, 8], F32)
        mxc = sb("mxc", [128, 1], F32)
        nmc = sb("nmc", [128, 1], F32)
        smc = sb("smc", [128, 1], F32)
        rsc = sb("rsc", [128, 1], F32)
        rmc = sb("rmc", [128, 1], F32)
        m0c = sb("m0c", [128, 1], F32)
        m1c = sb("m1c", [128, 1], F32)
        g0c = sb("g0c", [128, 1], F32)
        g1c = sb("g1c", [128, 1], F32)
        ccol = sb("ccol", [128, 1], F32)
        sumc = sb("sumc", [128, 1], F32)
        negm = sb("negm", [128, 1], F32)
        stdc = sb("stdc", [128, 1], F32)
        sqscr = sb("sqscr", [128, 512], F32)
        accc = sb("accc", [128, 4], F32)
        epsc = sb("epsc", [128, 1], F32)
        PA = st.enter_context(nc.psum_tensor("PA", [128, 512], F32))
        PB = st.enter_context(nc.psum_tensor("PB", [128, 512], F32))
        Ptf = st.enter_context(nc.psum_tensor("Ptf", [128, 128], F32))
        Ptb = st.enter_context(nc.psum_tensor("Ptb", [128, 128], BF16))
        Pg = st.enter_context(nc.psum_tensor("Pg", [128, 8], F32))
        S = st.enter_context(nc.semaphore("S"))
        block = st.enter_context(nc.Block())
        p = Prog()
        p.v(lambda e: e.memset(epsc[:], 1e-5))
        p.dma(identf[:], idf[:, :])
        p.dma(identb[:], idb[:, :])
        p.dma(o1r[:], onesr[:, :])
        p.dma(b0sb[:], b0r[:, :])
        p.dma(b1sb[:], b1r[:, :])
        for k in range(40):
            p.dma(wbuf[:], woT[128 * k:128 * (k + 1), :])
            p.v(lambda e, k=k: e.tensor_copy(out=wo_bf[:, 1024 * k:1024 * (k + 1)], in_=wbuf[:]))
        for k in range(8):
            p.dma(wbuf[:, 0:8], gw[128 * k:128 * (k + 1), :])
            p.v(lambda e, k=k: e.tensor_copy(out=gw_bf[:, 8 * k:8 * (k + 1)], in_=wbuf[:, 0:8]))
        for src, dst in ((w0T, w0_bf), (w1T, w1_bf)):
            for k in range(8):
                p.dma(wbuf[:], src[128 * k:128 * (k + 1), :])
                p.v(lambda e, dst=dst, k=k: e.tensor_copy(
                    out=dst[:, 1024 * k:1024 * (k + 1)], in_=wbuf[:]))
        for src, dst in ((og, ogbc), (ob, obbc)):
            p.dma(wbuf[0:1, :], src[:, :])
            for c in range(2):
                p.t(lambda e, c=c: e.matmul(out=PA[:], lhsT=o1r[0:1, :],
                                            rhs=wbuf[0:1, 512 * c:512 * (c + 1)],
                                            start=True, stop=True, skip_group_check=True))
                p.v(lambda e, dst=dst, c=c: e.tensor_copy(out=dst[:, 512 * c:512 * (c + 1)], in_=PA[:]))
        for tau in range(16):
            r0 = 128 * tau
            for k in range(8):
                p.dma(otb[:, 128 * k:128 * (k + 1)], oTs[128 * k:128 * (k + 1), r0:r0 + 128])
            p.dma(pgt[:], pgs[r0:r0 + 128, :])
            for m in range(32):
                p.t(lambda e, m=m: e.transpose(out=Ptb[:], in_=pgt[:, 128 * m:128 * (m + 1)],
                                               identity=identb[:]))
                p.v(lambda e, m=m: e.tensor_copy(out=pgT[:, 128 * m:128 * (m + 1)], in_=Ptb[:]))
            for E in range(2):
                PX = PA if E == 0 else PB
                for k in range(8):
                    p.t(lambda e, PX=PX, k=k, E=E: e.matmul(
                        out=PX[:], lhsT=otb[:, 128 * k:128 * (k + 1)],
                        rhs=wo_bf[:, 1024 * k + 512 * E:1024 * k + 512 * E + 512],
                        start=(k == 0), stop=False, skip_group_check=True))
                for m in range(32):
                    p.t(lambda e, PX=PX, m=m, E=E: e.matmul(
                        out=PX[:], lhsT=pgT[:, 128 * m:128 * (m + 1)],
                        rhs=wo_bf[:, 1024 * (8 + m) + 512 * E:1024 * (8 + m) + 512 * E + 512],
                        start=False, stop=(m == 31), skip_group_check=True))
                p.v(lambda e, PX=PX, E=E: e.tensor_copy(out=obuf[:, 512 * E:512 * E + 512], in_=PX[:]))
            ln_plain(p, obuf[:], 1024, sumc, negm, stdc, sqscr, accc, epsc)
            p.v(lambda e: e.tensor_tensor(out=obuf[:], in0=obuf[:], in1=ogbc[:], op=OP.mult))
            p.v(lambda e: e.tensor_tensor(out=obuf[:], in0=obuf[:], in1=obbc[:], op=OP.add))
            p.dma(xt[:], x2s[r0:r0 + 128, :])
            p.v(lambda e: e.tensor_tensor(out=x3[:], in0=xt[:], in1=obuf[:], op=OP.add))
            ln_plain(p, x3[:], 1024, sumc, negm, stdc, sqscr, accc, epsc)
            for k in range(8):
                p.t(lambda e, k=k: e.transpose(out=Ptf[:], in_=x3[:, 128 * k:128 * (k + 1)],
                                               identity=identf[:]))
                p.v(lambda e, k=k: e.tensor_copy(out=x3T[:, 128 * k:128 * (k + 1)], in_=Ptf[:]))
            for k in range(8):
                p.t(lambda e, k=k: e.matmul(out=Pg[:], lhsT=x3T[:, 128 * k:128 * (k + 1)],
                                            rhs=gw_bf[:, 8 * k:8 * (k + 1)],
                                            start=(k == 0), stop=(k == 7), skip_group_check=True))
            p.v(lambda e: e.reduce_max(out=mxc[:], in_=Pg[:], axis=AX))
            p.v(lambda e: e.tensor_scalar_mul(out=nmc[:], in0=mxc[:], scalar1=-1.0))
            p.s(lambda e: e.activation(out=wsm[:], in_=Pg[:], func=AF.Exp, bias=nmc[:],
                                       accum_out=smc[:]))
            p.v(lambda e: e.reciprocal(out=rsc[:], in_=smc[:]))
            p.v(lambda e: e.tensor_scalar_mul(out=wsm[:], in0=wsm[:], scalar1=rsc[:]))
            p.v(lambda e: e.reduce_max(out=rmc[:], in_=wsm[:, 1:8], axis=AX))
            p.v(lambda e: e.tensor_tensor(out=m0c[:], in0=wsm[:, 0:1], in1=rmc[:], op=OP.is_gt))
            p.v(lambda e: e.tensor_scalar(out=gb8[:], in0=wsm[:], scalar1=wsm[:, 1:2],
                                          scalar2=None, op0=OP.is_gt))
            p.v(lambda e: e.reduce_sum(out=ccol[:], in_=gb8[:], axis=AX))
            p.v(lambda e: e.tensor_scalar(out=m1c[:], in0=ccol[:], scalar1=1.0,
                                          scalar2=None, op0=OP.is_equal))
            p.v(lambda e: e.tensor_tensor(out=g0c[:], in0=wsm[:, 0:1], in1=m0c[:], op=OP.mult))
            p.v(lambda e: e.tensor_tensor(out=g1c[:], in0=wsm[:, 1:2], in1=m1c[:], op=OP.mult))
            for ei, (wbf, bsb, gcolv) in enumerate(((w0_bf, b0sb, g0c), (w1_bf, b1sb, g1c))):
                for E in range(2):
                    PX = PA if E == 0 else PB
                    for k in range(8):
                        p.t(lambda e, PX=PX, wbf=wbf, k=k, E=E: e.matmul(
                            out=PX[:], lhsT=x3T[:, 128 * k:128 * (k + 1)],
                            rhs=wbf[:, 1024 * k + 512 * E:1024 * k + 512 * E + 512],
                            start=(k == 0), stop=False, skip_group_check=True))
                    p.t(lambda e, PX=PX, bsb=bsb, E=E: e.matmul(
                        out=PX[:], lhsT=o1r[0:1, :], rhs=bsb[0:1, 512 * E:512 * E + 512],
                        start=False, stop=True, skip_group_check=True))
                    p.s(lambda e, PX=PX, E=E: e.activation(
                        out=ebuf[:, 512 * E:512 * E + 512], in_=PX[:], func=AF.Silu))
                if ei == 0:
                    p.v(lambda e, gcolv=gcolv: e.tensor_scalar_mul(out=obuf[:], in0=ebuf[:],
                                                                   scalar1=gcolv[:]))
                else:
                    p.v(lambda e, gcolv=gcolv: e.scalar_tensor_tensor(
                        out=obuf[:], in0=ebuf[:], scalar=gcolv[:], in1=obuf[:],
                        op0=OP.mult, op1=OP.add))
            p.v(lambda e: e.tensor_tensor(out=x3[:], in0=x3[:], in1=obuf[:], op=OP.add))
            ln_plain(p, x3[:], 1024, sumc, negm, stdc, sqscr, accc, epsc)
            p.dma(yo[r0:r0 + 128, :], x3[:])
        p.emit(nc, S, block)


def kernel(L0_attn_x, L0_attn_a, L0_attn_b, L0_attn_w_out, L0_attn_b_out,
           L0_ffn_fast_b, L0_ffn_w_fc, L0_ffn_w_proj, L0_ffn_u, L0_ffn_v, L0_ffn_gate,
           L1_attn_in_ln_g, L1_attn_in_ln_b, L1_attn_W_in, L1_attn_mid_ln_g,
           L1_attn_mid_ln_b, L1_attn_slopes, L1_attn_W_out, L1_attn_out_ln_g,
           L1_attn_out_ln_b, L1_ffn_gate_w, L1_ffn_w0, L1_ffn_b0, L1_ffn_w1, L1_ffn_b1):
    x = f32c(L0_attn_x)
    ident = np.eye(128, dtype=np.float32)
    identb = np.eye(128, dtype=np.float32).astype(BF)
    ones_r = np.ones((1, 128), dtype=np.float32)
    ones_c = np.ones((128, 1), dtype=np.float32)

    maps1 = []
    for c in range(NCORE):
        ck = slice(128 * c, 128 * (c + 1))
        maps1.append({
            "xs": f32c(x[:, :, ck]),
            "a_s": f32c(L0_attn_a[ck]).reshape(128, 1),
            "b_s": f32c(L0_attn_b[ck]).reshape(128, 1),
            "idf": ident,
        })
    r1 = run(build_scan, maps1)
    yT_full = np.stack([np.concatenate([r1[c]["yT"][b] for c in range(NCORE)], axis=0)
                        for b in range(B)], axis=0)

    shared2 = {
        "w_outT": f32c(L0_attn_w_out.T),
        "b_row": f32c(L0_attn_b_out).reshape(1, D),
        "w_fcT": f32c(L0_ffn_w_fc.T),
        "w_projT": f32c(L0_ffn_w_proj.T),
        "uT": f32c(L0_ffn_u.T),
        "vT": f32c(L0_ffn_v.T),
        "fbT": f32c(L0_ffn_fast_b.T),
        "gate": f32c(L0_ffn_gate).reshape(1, 1),
        "onesr": ones_r, "onesc": ones_c, "idf": ident,
    }
    maps2 = []
    for c in range(NCORE):
        b = c // 4
        t0 = 2048 * (c % 4)
        m = dict(shared2)
        m["xs"] = f32c(x[b, t0:t0 + TPC])
        m["yTs"] = f32c(yT_full[b][:, t0:t0 + TPC])
        maps2.append(m)
    r2 = run(build_l2, maps2)
    x2 = np.concatenate([r2[c]["x2o"] for c in range(NCORE)], axis=0).reshape(B, T, D)

    shared3 = {
        "g_in": f32c(L1_attn_in_ln_g).reshape(D, 1),
        "b_in": f32c(L1_attn_in_ln_b).reshape(D, 1),
        "winT": f32c(L1_attn_W_in.T),
        "mg": f32c(L1_attn_mid_ln_g).reshape(1, 7168),
        "mb": f32c(L1_attn_mid_ln_b).reshape(1, 7168),
        "onesr": ones_r, "idf": ident,
    }
    maps3 = []
    for c in range(NCORE):
        b = c // 4
        t0 = 2048 * (c % 4)
        m = dict(shared3)
        if t0 == 0:
            xh = np.concatenate([x2[b, 0:128], x2[b, 0:TPC]], axis=0)
            hcol = np.zeros((128, 1), dtype=np.float32)
        else:
            xh = x2[b, t0 - 128:t0 + TPC]
            hcol = np.ones((128, 1), dtype=np.float32)
        m["x2h"] = f32c(xh)
        m["halo"] = hcol
        maps3.append(m)
    r3 = run(build_l3, maps3)
    qkv = np.concatenate([r3[c]["qkvo"] for c in range(NCORE)], axis=0).reshape(B, T, 3072)
    pg = np.concatenate([r3[c]["pgo"] for c in range(NCORE)], axis=0).reshape(B, T, 4096)

    r_idx = np.arange(128)
    c_idx = np.arange(512)
    caus = np.concatenate(
        [np.where(r_idx[:, None] > c_idx[None, :] - 128 * v, np.float32(-1e10), np.float32(0.0))
         for v in range(4)], axis=1).astype(np.float32)
    iota2d = (r_idx[:, None] + 128 * np.arange(64)[None, :]).astype(np.float32)
    slopes = f32c(L1_attn_slopes)
    maps4 = []
    for c in range(NCORE):
        prs = [(pp // 16, pp % 16) for pp in range(4 * c, 4 * c + 4)]
        maps4.append({
            "q_in": bfc(np.stack([qkv[b, :, 64 * h:64 * h + 64] for (b, h) in prs])),
            "k_in": bfc(np.stack([qkv[b, :, 1024 + 64 * h:1024 + 64 * h + 64] for (b, h) in prs])),
            "v_in": bfc(np.stack([qkv[b, :, 2048 + 64 * h:2048 + 64 * h + 64] for (b, h) in prs])),
            "sl_in": np.stack([slopes[h:h + 1] for (b, h) in prs]),
            "iota": iota2d, "caus": caus, "idb": identb, "onesr": ones_r,
        })
    r4 = run(build_l4, maps4)
    oT_full = np.zeros((B, D, T), dtype=BF)
    for c in range(NCORE):
        for i, pp in enumerate(range(4 * c, 4 * c + 4)):
            b, h = pp // 16, pp % 16
            oT_full[b, 64 * h:64 * h + 64, :] = r4[c]["oT"][i]

    shared5 = {
        "woT": f32c(L1_attn_W_out.T),
        "og": f32c(L1_attn_out_ln_g).reshape(1, D),
        "ob": f32c(L1_attn_out_ln_b).reshape(1, D),
        "gw": f32c(L1_ffn_gate_w),
        "w0T": f32c(L1_ffn_w0.T),
        "b0r": f32c(L1_ffn_b0).reshape(1, D),
        "w1T": f32c(L1_ffn_w1.T),
        "b1r": f32c(L1_ffn_b1).reshape(1, D),
        "onesr": ones_r, "idf": ident, "idb": identb,
    }
    maps5 = []
    for c in range(NCORE):
        b = c // 4
        t0 = 2048 * (c % 4)
        m = dict(shared5)
        m["oTs"] = bfc(oT_full[b][:, t0:t0 + TPC])
        m["pgs"] = bfc(pg[b, t0:t0 + TPC])
        m["x2s"] = f32c(x2[b, t0:t0 + TPC])
        maps5.append(m)
    r5 = run(build_l5, maps5)
    out = np.concatenate([r5[c]["yo"] for c in range(NCORE)], axis=0).reshape(B, T, D)
    return out.astype(np.float32)


# version 04
# speedup vs baseline: 1.2198x; 1.0725x over previous
import numpy as np
import ml_dtypes
import concourse.bass as bass
import concourse.mybir as mybir
from concourse.bass_utils import run_bass_kernel_spmd
from contextlib import ExitStack

F32 = mybir.dt.float32
BF16 = mybir.dt.bfloat16
AF = mybir.ActivationFunctionType
OP = mybir.AluOpType
AX = mybir.AxisListType.X
BF = ml_dtypes.bfloat16

NCORE = 8
B, T, D = 2, 8192, 1024
TPC = (B * T) // NCORE


class Prog:

    def __init__(self):
        self.items = []

    def t(self, fn):
        self.items.append(("tensor", fn, 1))

    def v(self, fn):
        self.items.append(("vector", fn, 1))

    def s(self, fn):
        self.items.append(("scalar", fn, 1))

    def dma(self, out, in_):
        self.items.append(("sync", lambda e, o=out, i=in_: e.dma_start(out=o, in_=i), 16))

    def emit(self, nc, S, block):
        items = self.items
        cum = [0]
        for (_, _, inc) in items:
            cum.append(cum[-1] + inc)
        total = cum[-1]
        relaxed = {"tensor", "sync"}
        wtgt = [0] * len(items)
        run0 = 0
        for i, (en, _, _) in enumerate(items):
            if i > 0 and en != items[i - 1][0]:
                run0 = i
            wtgt[i] = cum[run0] if en in relaxed else cum[i]
        engines = ["sync", "tensor", "vector", "scalar"]

        def mk(name):
            def f(eng):
                last = -1
                for i, (en, fn, inc) in enumerate(items):
                    if en != name:
                        continue
                    if wtgt[i] > last:
                        eng.wait_ge(S, wtgt[i])
                        last = wtgt[i]
                    fn(eng).then_inc(S, inc)
                if name == "sync":
                    eng.wait_ge(S, total)
            return f

        for en in engines:
            getattr(block, en)(mk(en))


def ln_plain(p, buf, W, sumc, negm, stdc, sqscr, accc, epsc):
    nch = (W + 511) // 512
    p.v(lambda e, b=buf: e.reduce_sum(out=sumc[:], in_=b, axis=AX))
    p.v(lambda e: e.tensor_scalar_mul(out=negm[:], in0=sumc[:], scalar1=-1.0 / W))
    p.s(lambda e, b=buf: e.activation(out=b, in_=b, func=AF.Identity, bias=negm[:]))
    for c in range(nch):
        w0 = 512 * c
        w1 = min(W, w0 + 512)
        p.s(lambda e, b=buf, w0=w0, w1=w1, c=c: e.activation(
            out=sqscr[:, 0:w1 - w0], in_=b[:, w0:w1], func=AF.Square,
            accum_out=accc[:, c:c + 1]))
    p.v(lambda e, n=nch: e.reduce_sum(out=sumc[:], in_=accc[:, 0:n], axis=AX))
    p.s(lambda e: e.activation(out=stdc[:], in_=sumc[:], func=AF.Sqrt, scale=1.0 / W, bias=epsc[:]))
    p.v(lambda e: e.reciprocal(out=stdc[:], in_=stdc[:]))
    p.v(lambda e, b=buf: e.tensor_scalar_mul(out=b, in0=b, scalar1=stdc[:]))


def run(build_fn, in_maps):
    nc = bass.Bass("TRN2", target_bir_lowering=False)
    build_fn(nc)
    res = run_bass_kernel_spmd(nc, in_maps, core_ids=list(range(NCORE)))
    return res.results


def f32c(a):
    return np.ascontiguousarray(a, dtype=np.float32)


def bfc(a):
    return np.ascontiguousarray(a)


def build_scan(nc):
    xs = nc.dram_tensor("xs", [B, T, 128], F32, kind="ExternalInput").ap()
    a_s = nc.dram_tensor("a_s", [128, 1], F32, kind="ExternalInput").ap()
    b_s = nc.dram_tensor("b_s", [128, 1], F32, kind="ExternalInput").ap()
    idf = nc.dram_tensor("idf", [128, 128], F32, kind="ExternalInput").ap()
    yT = nc.dram_tensor("yT", [B, 128, T], F32, kind="ExternalOutput").ap()
    with ExitStack() as st:
        xT = st.enter_context(nc.sbuf_tensor("xT", [128, T], F32))
        bx = st.enter_context(nc.sbuf_tensor("bx", [128, T], F32))
        abc = st.enter_context(nc.sbuf_tensor("abc", [128, T], F32))
        acol = st.enter_context(nc.sbuf_tensor("acol", [128, 1], F32))
        bcol = st.enter_context(nc.sbuf_tensor("bcol", [128, 1], F32))
        ident = st.enter_context(nc.sbuf_tensor("ident", [128, 128], F32))
        tin = st.enter_context(nc.sbuf_tensor("tin", [128, 128], F32))
        tp = st.enter_context(nc.psum_tensor("tp", [128, 128], F32))
        S = st.enter_context(nc.semaphore("S"))
        block = st.enter_context(nc.Block())
        p = Prog()
        p.dma(ident[:], idf[:, :])
        p.dma(acol[:], a_s[:, :])
        p.dma(bcol[:], b_s[:, :])
        p.s(lambda e: e.activation(out=acol[:], in_=acol[:], func=AF.Tanh))
        p.v(lambda e: e.memset(abc[:], 1.0))
        p.v(lambda e: e.tensor_scalar_mul(out=abc[:], in0=abc[:], scalar1=acol[:]))
        for b in range(B):
            for j in range(T // 128):
                p.dma(tin[:], xs[b, 128 * j:128 * (j + 1), :])
                p.t(lambda e: e.transpose(out=tp[:], in_=tin[:], identity=ident[:]))
                p.v(lambda e, j=j: e.tensor_copy(out=xT[:, 128 * j:128 * (j + 1)], in_=tp[:]))
            p.v(lambda e: e.tensor_scalar_mul(out=bx[:], in0=xT[:], scalar1=bcol[:]))
            for c in range(4):
                c0 = 2048 * c
                if c == 0:
                    p.v(lambda e: e.tensor_tensor_scan(
                        out=xT[:, 0:2048], data0=abc[:, 0:2048], data1=bx[:, 0:2048],
                        initial=0.0, op0=OP.mult, op1=OP.add))
                else:
                    p.v(lambda e, c0=c0: e.tensor_tensor_scan(
                        out=xT[:, c0:c0 + 2048], data0=abc[:, c0:c0 + 2048],
                        data1=bx[:, c0:c0 + 2048], initial=xT[:, c0 - 1:c0],
                        op0=OP.mult, op1=OP.add))
            p.dma(yT[b, :, :], xT[:])
        p.emit(nc, S, block)


def build_l2(nc):
    xs = nc.dram_tensor("xs", [TPC, D], F32, kind="ExternalInput").ap()
    yTs = nc.dram_tensor("yTs", [D, TPC], F32, kind="ExternalInput").ap()
    w_outT = nc.dram_tensor("w_outT", [D, D], F32, kind="ExternalInput").ap()
    b_row = nc.dram_tensor("b_row", [1, D], F32, kind="ExternalInput").ap()
    w_fcT = nc.dram_tensor("w_fcT", [D, 4096], F32, kind="ExternalInput").ap()
    w_projT = nc.dram_tensor("w_projT", [4096, D], F32, kind="ExternalInput").ap()
    uT = nc.dram_tensor("uT", [16, 4096], F32, kind="ExternalInput").ap()
    vT = nc.dram_tensor("vT", [D, 16], F32, kind="ExternalInput").ap()
    fbT = nc.dram_tensor("fbT", [16, 16], F32, kind="ExternalInput").ap()
    gate = nc.dram_tensor("gate", [1, 1], F32, kind="ExternalInput").ap()
    onesr = nc.dram_tensor("onesr", [1, 128], F32, kind="ExternalInput").ap()
    onesc = nc.dram_tensor("onesc", [128, 1], F32, kind="ExternalInput").ap()
    idf = nc.dram_tensor("idf", [128, 128], F32, kind="ExternalInput").ap()
    w2hbm = nc.dram_tensor("w2hbm", [32, 128, D], BF16, kind="Internal").ap()
    x2o = nc.dram_tensor("x2o", [TPC, D], F32, kind="ExternalOutput").ap()
    with ExitStack() as st:
        sb = lambda n, sh, dt=F32: st.enter_context(nc.sbuf_tensor(n, sh, dt))
        w1e = sb("w1e", [128, 8 * 4096], BF16)
        scbc = sb("scbc", [128, 4096], F32)
        scbc2 = sb("scbc2", [128, 1024], F32)
        shalf = sb("shalf", [128, 1024], F32)
        scrow = sb("scrow", [1, 4096], F32)
        wbuf = sb("wbuf", [128, 1024], F32)
        zb = sb("zb", [128, 1024], F32)
        t1b = sb("t1b", [128, 1024], F32)
        stg = sb("stg", [128, 1024], BF16)
        wo_bf = sb("wo_bf", [128, 8 * 1024], BF16)
        vt_bf = sb("vt_bf", [128, 128], BF16)
        ut_bf = sb("ut_bf", [16, 4096], BF16)
        fb_bf = sb("fb_bf", [16, 16], BF16)
        g16 = sb("g16", [16, 1], F32)
        brow = sb("brow", [1, 1024], F32)
        o1r = sb("o1r", [1, 128], F32)
        o1c = sb("o1c", [128, 1], F32)
        ident = sb("ident", [128, 128], F32)
        gsb = sb("gsb", [1, 1], F32)
        X = sb("X", [128, 2048], F32)
        X1 = sb("X1", [128, 2048], F32)
        ytf = sb("ytf", [128, 2048], F32)
        yt_bf = sb("yt_bf", [128, 2048], BF16)
        x1T = sb("x1T", [128, 2048], BF16)
        rsq = sb("rsq", [128, 8192], BF16)
        rl = sb("rl", [128, 256], F32)
        t1sb = sb("t1sb", [16, 256], BF16)
        t2s = sb("t2s", [16, 256], BF16)
        w2buf = sb("w2buf", [128, 1024], BF16)
        sumc = sb("sumc", [128, 1], F32)
        negm = sb("negm", [128, 1], F32)
        stdc = sb("stdc", [128, 1], F32)
        sqscr = sb("sqscr", [128, 512], F32)
        accc = sb("accc", [128, 4], F32)
        epsc = sb("epsc", [128, 1], F32)
        ps = lambda n, sh, dt=F32: st.enter_context(nc.psum_tensor(n, sh, dt))
        P0 = ps("P0", [128, 512]); P1 = ps("P1", [128, 512])
        P2 = ps("P2", [128, 512]); P3 = ps("P3", [128, 512])
        Pt = ps("Pt", [128, 128]); Plr = ps("Plr", [16, 256])
        P2lr = ps("P2lr", [16, 256]); Ph = ps("Ph", [128, 512])
        POUT = [P0, P1, P2, P3]
        S = st.enter_context(nc.semaphore("S"))
        block = st.enter_context(nc.Block())
        p = Prog()
        p.v(lambda e: e.memset(epsc[:], 1e-5))
        p.dma(ident[:], idf[:, :])
        p.dma(o1r[:], onesr[:, :])
        p.dma(o1c[:], onesc[:, :])
        p.dma(brow[:], b_row[:, :])
        p.dma(gsb[:], gate[:, :])
        p.t(lambda e: e.matmul(out=Plr[0:16, 0:1], lhsT=o1r[0:1, 0:16], rhs=gsb[0:1, 0:1],
                               start=True, stop=True, skip_group_check=True))
        p.v(lambda e: e.tensor_copy(out=g16[:], in_=Plr[0:16, 0:1]))
        for c8 in range(8):
            for k in range(8):
                p.dma(wbuf[:, 0:512], w_fcT[128 * k:128 * (k + 1), 512 * c8:512 * (c8 + 1)])
                p.s(lambda e: e.activation(out=zb[:, 0:512], in_=wbuf[:, 0:512], func=AF.Abs))
                p.t(lambda e, k=k: e.matmul(out=P0[0:1, :], lhsT=o1c[:, 0:1], rhs=zb[:, 0:512],
                                            start=(k == 0), stop=(k == 7), skip_group_check=True))
            p.s(lambda e, c8=c8: e.activation(out=scrow[0:1, 512 * c8:512 * (c8 + 1)],
                                              in_=P0[0:1, :], func=AF.Identity, scale=1.0 / 1024))
        p.v(lambda e: e.tensor_scalar_max(out=scrow[:], in0=scrow[:], scalar1=1e-5))
        for c8 in range(8):
            p.t(lambda e, c8=c8: e.matmul(out=P0[:], lhsT=o1r[0:1, :],
                                          rhs=scrow[0:1, 512 * c8:512 * (c8 + 1)],
                                          start=True, stop=True, skip_group_check=True))
            p.v(lambda e, c8=c8: e.tensor_copy(out=scbc[:, 512 * c8:512 * (c8 + 1)], in_=P0[:]))
        for k in range(8):
            for c in range(4):
                cc = 1024 * c
                p.dma(wbuf[:], w_fcT[128 * k:128 * (k + 1), cc:cc + 1024])
                p.v(lambda e, cc=cc: e.tensor_scalar_mul(out=shalf[:], in0=scbc[:, cc:cc + 1024],
                                                         scalar1=0.5))
                p.s(lambda e: e.activation(out=zb[:], in_=wbuf[:], func=AF.Abs))
                p.v(lambda e: e.tensor_tensor(out=t1b[:], in0=zb[:], in1=shalf[:], op=OP.is_ge))
                p.s(lambda e: e.activation(out=zb[:], in_=wbuf[:], func=AF.Sign))
                p.v(lambda e: e.tensor_tensor(out=t1b[:], in0=t1b[:], in1=zb[:], op=OP.mult))
                p.v(lambda e, k=k, cc=cc: e.tensor_tensor(
                    out=w1e[:, 4096 * k + cc:4096 * k + cc + 1024], in0=t1b[:],
                    in1=scbc[:, cc:cc + 1024], op=OP.mult))
        for c2 in range(2):
            for k2 in range(32):
                p.dma(wbuf[:, 0:512], w_projT[128 * k2:128 * (k2 + 1), 512 * c2:512 * (c2 + 1)])
                p.s(lambda e: e.activation(out=zb[:, 0:512], in_=wbuf[:, 0:512], func=AF.Abs))
                p.t(lambda e, k2=k2: e.matmul(out=P0[0:1, :], lhsT=o1c[:, 0:1], rhs=zb[:, 0:512],
                                              start=(k2 == 0), stop=(k2 == 31), skip_group_check=True))
            p.s(lambda e, c2=c2: e.activation(out=scrow[0:1, 512 * c2:512 * (c2 + 1)],
                                              in_=P0[0:1, :], func=AF.Identity, scale=1.0 / 4096))
        p.v(lambda e: e.tensor_scalar_max(out=scrow[:, 0:1024], in0=scrow[:, 0:1024], scalar1=1e-5))
        for c2 in range(2):
            p.t(lambda e, c2=c2: e.matmul(out=P0[:], lhsT=o1r[0:1, :],
                                          rhs=scrow[0:1, 512 * c2:512 * (c2 + 1)],
                                          start=True, stop=True, skip_group_check=True))
            p.v(lambda e, c2=c2: e.tensor_copy(out=scbc2[:, 512 * c2:512 * (c2 + 1)], in_=P0[:]))
        p.v(lambda e: e.tensor_scalar_mul(out=shalf[:], in0=scbc2[:], scalar1=0.5))
        for k2 in range(32):
            p.dma(wbuf[:], w_projT[128 * k2:128 * (k2 + 1), :])
            p.s(lambda e: e.activation(out=zb[:], in_=wbuf[:], func=AF.Abs))
            p.v(lambda e: e.tensor_tensor(out=t1b[:], in0=zb[:], in1=shalf[:], op=OP.is_ge))
            p.s(lambda e: e.activation(out=zb[:], in_=wbuf[:], func=AF.Sign))
            p.v(lambda e: e.tensor_tensor(out=t1b[:], in0=t1b[:], in1=zb[:], op=OP.mult))
            p.v(lambda e: e.tensor_tensor(out=stg[:], in0=t1b[:], in1=scbc2[:], op=OP.mult))
            p.dma(w2hbm[k2, :, :], stg[:])
        for k in range(8):
            p.dma(wbuf[:], w_outT[128 * k:128 * (k + 1), :])
            p.v(lambda e, k=k: e.tensor_copy(out=wo_bf[:, 1024 * k:1024 * (k + 1)], in_=wbuf[:]))
        for k in range(8):
            p.dma(wbuf[:, 0:16], vT[128 * k:128 * (k + 1), :])
            p.v(lambda e, k=k: e.tensor_copy(out=vt_bf[:, 16 * k:16 * (k + 1)], in_=wbuf[:, 0:16]))
        for c in range(4):
            p.dma(wbuf[0:16, :], uT[:, 1024 * c:1024 * (c + 1)])
            p.v(lambda e, c=c: e.tensor_copy(out=ut_bf[:, 1024 * c:1024 * (c + 1)],
                                             in_=wbuf[0:16, :]))
        p.dma(wbuf[0:16, 0:16], fbT[:, :])
        p.v(lambda e: e.tensor_copy(out=fb_bf[:], in_=wbuf[0:16, 0:16]))
        for s8 in range(8):
            t0 = 256 * s8
            for tl in range(2):
                p.dma(X[:, 1024 * tl:1024 * (tl + 1)], xs[t0 + 128 * tl:t0 + 128 * (tl + 1), :])
            for k in range(8):
                p.dma(ytf[:, 256 * k:256 * (k + 1)], yTs[128 * k:128 * (k + 1), t0:t0 + 256])
            p.v(lambda e: e.tensor_copy(out=yt_bf[:], in_=ytf[:]))
            for tl in range(2):
                for E in range(2):
                    PM = POUT[2 * tl + E]
                    for k in range(8):
                        p.t(lambda e, PM=PM, k=k, tl=tl, E=E: e.matmul(
                            out=PM[:], lhsT=yt_bf[:, 256 * k + 128 * tl:256 * k + 128 * tl + 128],
                            rhs=wo_bf[:, 1024 * k + 512 * E:1024 * k + 512 * E + 512],
                            start=(k == 0), stop=False, skip_group_check=True))
                    p.t(lambda e, PM=PM, E=E: e.matmul(
                        out=PM[:], lhsT=o1r[0:1, :], rhs=brow[0:1, 512 * E:512 * E + 512],
                        start=False, stop=True, skip_group_check=True))
                    p.v(lambda e, PM=PM, tl=tl, E=E: e.tensor_tensor(
                        out=X1[:, 1024 * tl + 512 * E:1024 * tl + 512 * E + 512],
                        in0=X[:, 1024 * tl + 512 * E:1024 * tl + 512 * E + 512],
                        in1=PM[:], op=OP.add))
                ln_plain(p, X1[:, 1024 * tl:1024 * (tl + 1)], 1024, sumc, negm, stdc, sqscr, accc, epsc)
                for k in range(8):
                    p.t(lambda e, tl=tl, k=k: e.transpose(
                        out=Pt[:], in_=X1[:, 1024 * tl + 128 * k:1024 * tl + 128 * k + 128],
                        identity=ident[:]))
                    p.v(lambda e, tl=tl, k=k: e.tensor_copy(
                        out=x1T[:, 256 * k + 128 * tl:256 * k + 128 * tl + 128], in_=Pt[:]))
            for k in range(8):
                p.t(lambda e, k=k: e.matmul(out=Plr[:], lhsT=vt_bf[:, 16 * k:16 * (k + 1)],
                                            rhs=x1T[:, 256 * k:256 * (k + 1)],
                                            start=(k == 0), stop=(k == 7), skip_group_check=True))
            p.v(lambda e: e.tensor_copy(out=t1sb[:], in_=Plr[:]))
            p.t(lambda e: e.matmul(out=P2lr[:], lhsT=fb_bf[:], rhs=t1sb[:],
                                   start=True, stop=True, skip_group_check=True))
            p.v(lambda e: e.tensor_scalar_mul(out=t2s[:], in0=P2lr[:], scalar1=g16[:]))
            for H in range(32):
                for k in range(8):
                    p.t(lambda e, H=H, k=k: e.matmul(
                        out=Ph[:, 0:256], lhsT=w1e[:, 4096 * k + 128 * H:4096 * k + 128 * H + 128],
                        rhs=x1T[:, 256 * k:256 * (k + 1)],
                        start=(k == 0), stop=False, skip_group_check=True))
                p.t(lambda e, H=H: e.matmul(out=Ph[:, 0:256], lhsT=ut_bf[0:16, 128 * H:128 * H + 128],
                                            rhs=t2s[:], start=False, stop=True, skip_group_check=True))
                p.s(lambda e: e.activation(out=rl[:], in_=Ph[:, 0:256], func=AF.Relu))
                p.s(lambda e, H=H: e.activation(out=rsq[:, 256 * H:256 * (H + 1)], in_=rl[:],
                                                func=AF.Square))
            for k2 in range(32):
                p.dma(w2buf[:], w2hbm[k2, :, :])
                for tl in range(2):
                    for E in range(2):
                        p.t(lambda e, k2=k2, tl=tl, E=E: e.matmul(
                            out=POUT[2 * tl + E][:],
                            lhsT=rsq[:, 256 * k2 + 128 * tl:256 * k2 + 128 * tl + 128],
                            rhs=w2buf[:, 512 * E:512 * E + 512],
                            start=(k2 == 0), stop=(k2 == 31), skip_group_check=True))
            for tl in range(2):
                for E in range(2):
                    p.v(lambda e, tl=tl, E=E: e.tensor_tensor(
                        out=X[:, 1024 * tl + 512 * E:1024 * tl + 512 * E + 512],
                        in0=X1[:, 1024 * tl + 512 * E:1024 * tl + 512 * E + 512],
                        in1=POUT[2 * tl + E][:], op=OP.add))
                ln_plain(p, X[:, 1024 * tl:1024 * (tl + 1)], 1024, sumc, negm, stdc, sqscr, accc, epsc)
                p.dma(x2o[t0 + 128 * tl:t0 + 128 * (tl + 1), :], X[:, 1024 * tl:1024 * (tl + 1)])
        p.emit(nc, S, block)


def build_l3(nc):
    QKVP, HID = 7168, 1024
    x2h = nc.dram_tensor("x2h", [TPC + 128, D], F32, kind="ExternalInput").ap()
    halo = nc.dram_tensor("halo", [128, 1], F32, kind="ExternalInput").ap()
    g_in = nc.dram_tensor("g_in", [D, 1], F32, kind="ExternalInput").ap()
    b_in = nc.dram_tensor("b_in", [D, 1], F32, kind="ExternalInput").ap()
    winT = nc.dram_tensor("winT", [D, QKVP], F32, kind="ExternalInput").ap()
    mg = nc.dram_tensor("mg", [1, QKVP], F32, kind="ExternalInput").ap()
    mb = nc.dram_tensor("mb", [1, QKVP], F32, kind="ExternalInput").ap()
    onesr = nc.dram_tensor("onesr", [1, 128], F32, kind="ExternalInput").ap()
    idf = nc.dram_tensor("idf", [128, 128], F32, kind="ExternalInput").ap()
    qkvo = nc.dram_tensor("qkvo", [TPC, 3072], BF16, kind="ExternalOutput").ap()
    pgo = nc.dram_tensor("pgo", [TPC, 4096], BF16, kind="ExternalOutput").ap()
    with ExitStack() as st:
        sb = lambda n, sh, dt=F32: st.enter_context(nc.sbuf_tensor(n, sh, dt))
        win_bf = sb("win_bf", [128, 8 * QKVP], BF16)
        mgbc = sb("mgbc", [128, QKVP], BF16)
        mbbc = sb("mbbc", [128, QKVP], BF16)
        h_sb = sb("h_sb", [128, QKVP], F32)
        gcols = sb("gcols", [128, 8], F32)
        bcols = sb("bcols", [128, 8], F32)
        bhcols = sb("bhcols", [128, 8], F32)
        halocol = sb("halocol", [128, 1], F32)
        xt = sb("xt", [128, 1024], F32)
        xbuf = sb("xbuf", [128, 2048], BF16)
        egc = sb("egc", [128, 512], F32)
        pgb = sb("pgb", [128, 4096], BF16)
        o1r = sb("o1r", [1, 128], F32)
        ident = sb("ident", [128, 128], F32)
        sumc = sb("sumc", [128, 1], F32)
        negm = sb("negm", [128, 1], F32)
        stdc = sb("stdc", [128, 1], F32)
        sqscr = sb("sqscr", [128, 512], F32)
        accc = sb("accc", [128, 16], F32)
        epsc = sb("epsc", [128, 1], F32)
        Pm = st.enter_context(nc.psum_tensor("Pm", [128, 512], F32))
        Pt = st.enter_context(nc.psum_tensor("Pt", [128, 128], F32))
        S = st.enter_context(nc.semaphore("S"))
        block = st.enter_context(nc.Block())
        p = Prog()
        p.v(lambda e: e.memset(epsc[:], 1e-5))
        p.dma(ident[:], idf[:, :])
        p.dma(o1r[:], onesr[:, :])
        p.dma(halocol[:], halo[:, :])
        for k in range(8):
            p.dma(gcols[:, k:k + 1], g_in[128 * k:128 * (k + 1), :])
            p.dma(bcols[:, k:k + 1], b_in[128 * k:128 * (k + 1), :])
        p.v(lambda e: e.tensor_scalar_mul(out=bhcols[:], in0=bcols[:], scalar1=halocol[:]))
        for src, dst in ((mg, mgbc), (mb, mbbc)):
            p.dma(h_sb[0:1, :], src[:, :])
            for c in range(14):
                p.t(lambda e, c=c: e.matmul(out=Pm[:], lhsT=o1r[0:1, :],
                                            rhs=h_sb[0:1, 512 * c:512 * (c + 1)],
                                            start=True, stop=True, skip_group_check=True))
                p.v(lambda e, dst=dst, c=c: e.tensor_copy(out=dst[:, 512 * c:512 * (c + 1)], in_=Pm[:]))
        for k in range(8):
            p.dma(h_sb[:], winT[128 * k:128 * (k + 1), :])
            p.v(lambda e, k=k: e.tensor_copy(out=win_bf[:, QKVP * k:QKVP * (k + 1)], in_=h_sb[:]))
        for tau in range(-1, 16):
            r0 = 128 * (tau + 1)
            p.dma(xt[:], x2h[r0:r0 + 128, :])
            if tau == -1:
                p.v(lambda e: e.tensor_scalar_mul(out=xt[:], in0=xt[:], scalar1=halocol[:]))
            ln_plain(p, xt[:], 1024, sumc, negm, stdc, sqscr, accc, epsc)
            if tau >= 0:
                for k in range(8):
                    p.v(lambda e, k=k: e.tensor_copy(out=xbuf[:, 256 * k:256 * k + 128],
                                                     in_=xbuf[:, 256 * k + 128:256 * k + 256]))
            bsel = bhcols if tau == -1 else bcols
            for k in range(8):
                p.t(lambda e, k=k: e.transpose(out=Pt[:], in_=xt[:, 128 * k:128 * (k + 1)],
                                               identity=ident[:]))
                p.v(lambda e, k=k, bsel=bsel: e.tensor_scalar(
                    out=xbuf[:, 256 * k + 128:256 * k + 256], in0=Pt[:],
                    scalar1=gcols[:, k:k + 1], scalar2=bsel[:, k:k + 1],
                    op0=OP.mult, op1=OP.add))
            if tau < 0:
                continue
            shifts = [1, 3, 0, 0, 0, 0, 0, 0]
            for O in range(14):
                for k in range(8):
                    off = 256 * k + 128 - shifts[k]
                    p.t(lambda e, O=O, k=k, off=off: e.matmul(
                        out=Pm[:], lhsT=xbuf[:, off:off + 128],
                        rhs=win_bf[:, QKVP * k + 512 * O:QKVP * k + 512 * O + 512],
                        start=(k == 0), stop=(k == 7), skip_group_check=True))
                p.s(lambda e, O=O: e.activation(out=h_sb[:, 512 * O:512 * (O + 1)], in_=Pm[:],
                                                func=AF.Relu, accum_out=accc[:, O:O + 1]))
            p.v(lambda e: e.reduce_sum(out=sumc[:], in_=accc[:, 0:14], axis=AX))
            p.v(lambda e: e.tensor_scalar_mul(out=negm[:], in0=sumc[:], scalar1=-1.0 / QKVP))
            p.s(lambda e: e.activation(out=h_sb[:], in_=h_sb[:], func=AF.Identity, bias=negm[:]))
            for c in range(14):
                p.s(lambda e, c=c: e.activation(out=sqscr[:], in_=h_sb[:, 512 * c:512 * (c + 1)],
                                                func=AF.Square, accum_out=accc[:, c:c + 1]))
            p.v(lambda e: e.reduce_sum(out=sumc[:], in_=accc[:, 0:14], axis=AX))
            p.s(lambda e: e.activation(out=stdc[:], in_=sumc[:], func=AF.Sqrt,
                                       scale=1.0 / QKVP, bias=epsc[:]))
            p.v(lambda e: e.reciprocal(out=stdc[:], in_=stdc[:]))
            p.v(lambda e: e.tensor_scalar_mul(out=h_sb[:], in0=h_sb[:], scalar1=stdc[:]))
            p.v(lambda e: e.tensor_tensor(out=h_sb[:], in0=h_sb[:], in1=mgbc[:], op=OP.mult))
            p.v(lambda e: e.tensor_tensor(out=h_sb[:], in0=h_sb[:], in1=mbbc[:], op=OP.add))
            p.v(lambda e: e.tensor_copy(out=pgb[:, 0:3072], in_=h_sb[:, 0:3072]))
            p.dma(qkvo[128 * tau:128 * (tau + 1), :], pgb[:, 0:3072])
            for c in range(8):
                pc = 3072 + 512 * c
                p.s(lambda e, pc=pc: e.activation(out=egc[:], in_=h_sb[:, pc:pc + 512],
                                                  func=AF.Erf, scale=0.7071067811865476))
                p.v(lambda e: e.tensor_scalar(out=egc[:], in0=egc[:], scalar1=0.5, scalar2=0.5,
                                              op0=OP.mult, op1=OP.add))
                p.v(lambda e, pc=pc: e.tensor_tensor(out=egc[:], in0=egc[:],
                                                     in1=h_sb[:, pc:pc + 512], op=OP.mult))
                if c == 0:
                    p.v(lambda e: e.tensor_tensor(out=pgb[:, 1:512], in0=egc[:, 1:512],
                                                  in1=h_sb[:, 3072:3072 + 511], op=OP.mult))
                    p.v(lambda e: e.tensor_tensor(out=pgb[:, 0:1], in0=egc[:, 0:1],
                                                  in1=h_sb[:, 7167:7168], op=OP.mult))
                else:
                    p.v(lambda e, c=c, pc=pc: e.tensor_tensor(
                        out=pgb[:, 512 * c:512 * (c + 1)], in0=egc[:],
                        in1=h_sb[:, pc - 1:pc + 511], op=OP.mult))
            p.dma(pgo[128 * tau:128 * (tau + 1), :], pgb[:])
        p.emit(nc, S, block)


def build_l4(nc):
    L = T
    q_in = nc.dram_tensor("q_in", [4, L, 64], BF16, kind="ExternalInput").ap()
    k_in = nc.dram_tensor("k_in", [4, L, 64], BF16, kind="ExternalInput").ap()
    v_in = nc.dram_tensor("v_in", [4, L, 64], BF16, kind="ExternalInput").ap()
    sl_in = nc.dram_tensor("sl_in", [4, 1], F32, kind="ExternalInput").ap()
    iota = nc.dram_tensor("iota", [128, 64], F32, kind="ExternalInput").ap()
    caus = nc.dram_tensor("caus", [128, 2048], F32, kind="ExternalInput").ap()
    idb = nc.dram_tensor("idb", [128, 128], BF16, kind="ExternalInput").ap()
    onesr = nc.dram_tensor("onesr", [1, 128], F32, kind="ExternalInput").ap()
    oT = nc.dram_tensor("oT", [4, 64, L], BF16, kind="ExternalOutput").ap()
    with ExitStack() as st:
        sb = lambda n, sh, dt=F32: st.enter_context(nc.sbuf_tensor(n, sh, dt))
        qT = sb("qT", [64, L], BF16)
        kT = sb("kT", [64, L], BF16)
        vaug = sb("vaug", [128, 65 * 64], BF16)
        qbuf = sb("qbuf", [128, 4096], BF16)
        ssb = sb("ssb", [128, 512], F32)
        pbf = sb("pbf", [128, 512], BF16)
        Macc = sb("Macc", [128, 512], F32)
        mred = sb("mred", [64, 512], F32)
        negr = sb("negr", [1, 512], F32)
        nbc = sb("nbc", [128, 512], F32)
        nbcv = sb("nbcv", [128, 2048], F32)
        alall = sb("alall", [128, 64], F32)
        iot = sb("iot", [128, 64], F32)
        casb = sb("casb", [128, 2048], F32)
        o1r = sb("o1r", [1, 128], F32)
        idsb = sb("idsb", [128, 128], BF16)
        slsb = sb("slsb", [1, 1], F32)
        slcol = sb("slcol", [128, 1], F32)
        of32 = sb("of32", [64, 512], F32)
        obf = sb("obf", [64, 512], BF16)
        rrow = sb("rrow", [1, 512], F32)
        ps_s = st.enter_context(nc.psum_tensor("ps_s", [128, 512], F32))
        ps_o = st.enter_context(nc.psum_tensor("ps_o", [65, 512], F32))
        ps_b = st.enter_context(nc.psum_tensor("ps_b", [128, 512], F32))
        ps_t = st.enter_context(nc.psum_tensor("ps_t", [64, 128], BF16))
        ps_r = st.enter_context(nc.psum_tensor("ps_r", [64, 512], F32))
        ps_c = st.enter_context(nc.psum_tensor("ps_c", [128, 1], F32))
        S = st.enter_context(nc.semaphore("S"))
        block = st.enter_context(nc.Block())
        p = Prog()
        p.dma(iot[:], iota[:, :])
        p.dma(casb[:], caus[:, :])
        p.dma(idsb[:], idb[:, :])
        p.dma(o1r[:], onesr[:, :])
        for pi in range(4):
            p.dma(slsb[:], sl_in[pi:pi + 1, :])
            p.t(lambda e: e.matmul(out=ps_c[:], lhsT=o1r[0:1, :], rhs=slsb[0:1, 0:1],
                                   start=True, stop=True, skip_group_check=True))
            p.v(lambda e: e.tensor_copy(out=slcol[:], in_=ps_c[:]))
            p.v(lambda e: e.tensor_scalar_mul(out=alall[:], in0=iot[:], scalar1=slcol[:]))
            p.v(lambda e: e.memset(vaug[:], 1.0))
            for g in range(8):
                p.dma(vaug[:].rearrange("p (j c) -> p j c", c=65)[:, 8 * g:8 * (g + 1), 0:64],
                      v_in[pi].rearrange("(j p) c -> p j c", p=128)[:, 8 * g:8 * (g + 1), :])
            for src, dstT in ((q_in, qT), (k_in, kT)):
                for g in range(2):
                    p.dma(qbuf[:].rearrange("p (j c) -> p j c", c=64)[:, 0:32, :],
                          src[pi].rearrange("(j p) c -> p j c", p=128)[:, 32 * g:32 * (g + 1), :])
                    for jj in range(32):
                        j = 32 * g + jj
                        p.t(lambda e, jj=jj: e.transpose(out=ps_t[:], in_=qbuf[:, 64 * jj:64 * (jj + 1)],
                                                         identity=idsb[:]))
                        p.v(lambda e, j=j, dstT=dstT: e.tensor_copy(
                            out=dstT[:, 128 * j:128 * (j + 1)], in_=ps_t[:]))
            for Q in range(16):
                q0 = 512 * Q
                nk = 4 * Q + 4
                for j in range(nk):
                    p.t(lambda e, j=j, q0=q0: e.matmul(
                        out=ps_s[:], lhsT=kT[:, 128 * j:128 * (j + 1)], rhs=qT[:, q0:q0 + 512],
                        start=True, stop=True, skip_group_check=True))
                    p.s(lambda e, j=j: e.activation(out=ssb[:], in_=ps_s[:], func=AF.Identity,
                                                    bias=alall[:, j:j + 1], scale=0.125))
                    if j >= 4 * Q:
                        v_ = j - 4 * Q
                        p.v(lambda e, v_=v_: e.tensor_tensor(
                            out=ssb[:], in0=ssb[:], in1=casb[:, 512 * v_:512 * (v_ + 1)], op=OP.add))
                    if j == 0:
                        p.v(lambda e: e.tensor_copy(out=Macc[:], in_=ssb[:]))
                    else:
                        p.v(lambda e: e.tensor_tensor(out=Macc[:], in0=Macc[:], in1=ssb[:], op=OP.max))
                for sh in (64, 32, 16, 8, 4, 2, 1):
                    p.dma(mred[0:sh, :], Macc[sh:2 * sh, :])
                    p.v(lambda e, sh=sh: e.tensor_tensor(out=Macc[0:sh, :], in0=Macc[0:sh, :],
                                                         in1=mred[0:sh, :], op=OP.max))
                p.v(lambda e: e.tensor_scalar_mul(out=negr[:], in0=Macc[0:1, :], scalar1=-1.0))
                p.t(lambda e: e.matmul(out=ps_b[:], lhsT=o1r[0:1, :], rhs=negr[0:1, :],
                                       start=True, stop=True, skip_group_check=True))
                p.v(lambda e: e.tensor_copy(out=nbc[:], in_=ps_b[:]))
                for v_ in range(4):
                    p.v(lambda e, v_=v_: e.tensor_tensor(
                        out=nbcv[:, 512 * v_:512 * (v_ + 1)], in0=nbc[:],
                        in1=casb[:, 512 * v_:512 * (v_ + 1)], op=OP.add))
                for j in range(nk):
                    p.t(lambda e, j=j, q0=q0: e.matmul(
                        out=ps_s[:], lhsT=kT[:, 128 * j:128 * (j + 1)], rhs=qT[:, q0:q0 + 512],
                        start=True, stop=True, skip_group_check=True))
                    if j >= 4 * Q:
                        v_ = j - 4 * Q
                        p.v(lambda e, v_=v_: e.scalar_tensor_tensor(
                            out=ssb[:], in0=ps_s[:], scalar=0.125,
                            in1=nbcv[:, 512 * v_:512 * (v_ + 1)], op0=OP.mult, op1=OP.add))
                    else:
                        p.v(lambda e: e.scalar_tensor_tensor(
                            out=ssb[:], in0=ps_s[:], scalar=0.125, in1=nbc[:],
                            op0=OP.mult, op1=OP.add))
                    p.s(lambda e, j=j: e.activation(out=pbf[:], in_=ssb[:], func=AF.Exp,
                                                    bias=alall[:, j:j + 1]))
                    p.t(lambda e, j=j, nk=nk: e.matmul(
                        out=ps_o[:], lhsT=vaug[:, 65 * j:65 * (j + 1)], rhs=pbf[:],
                        start=(j == 0), stop=(j == nk - 1), skip_group_check=True))
                p.v(lambda e: e.reciprocal(out=rrow[:], in_=ps_o[64:65, :]))
                p.t(lambda e: e.matmul(out=ps_r[:], lhsT=o1r[0:1, 0:64], rhs=rrow[0:1, :],
                                       start=True, stop=True, skip_group_check=True))
                p.v(lambda e: e.tensor_copy(out=of32[:], in_=ps_o[0:64, :]))
                p.v(lambda e: e.tensor_tensor(out=obf[:], in0=of32[:], in1=ps_r[:], op=OP.mult))
                p.dma(oT[pi, :, q0:q0 + 512], obf[:])
        p.emit(nc, S, block)


def build_l5(nc):
    VP = 5120
    oTs = nc.dram_tensor("oTs", [D, TPC], BF16, kind="ExternalInput").ap()
    pgs = nc.dram_tensor("pgs", [TPC, 4096], BF16, kind="ExternalInput").ap()
    x2s = nc.dram_tensor("x2s", [TPC, D], F32, kind="ExternalInput").ap()
    woT = nc.dram_tensor("woT", [VP, D], F32, kind="ExternalInput").ap()
    og = nc.dram_tensor("og", [1, D], F32, kind="ExternalInput").ap()
    ob = nc.dram_tensor("ob", [1, D], F32, kind="ExternalInput").ap()
    gw = nc.dram_tensor("gw", [D, 8], F32, kind="ExternalInput").ap()
    w0T = nc.dram_tensor("w0T", [D, D], F32, kind="ExternalInput").ap()
    b0r = nc.dram_tensor("b0r", [1, D], F32, kind="ExternalInput").ap()
    w1T = nc.dram_tensor("w1T", [D, D], F32, kind="ExternalInput").ap()
    b1r = nc.dram_tensor("b1r", [1, D], F32, kind="ExternalInput").ap()
    onesr = nc.dram_tensor("onesr", [1, 128], F32, kind="ExternalInput").ap()
    idf = nc.dram_tensor("idf", [128, 128], F32, kind="ExternalInput").ap()
    idb = nc.dram_tensor("idb", [128, 128], BF16, kind="ExternalInput").ap()
    yo = nc.dram_tensor("yo", [TPC, D], F32, kind="ExternalOutput").ap()
    with ExitStack() as st:
        sb = lambda n, sh, dt=F32: st.enter_context(nc.sbuf_tensor(n, sh, dt))
        wo_bf = sb("wo_bf", [128, 40 * 1024], BF16)
        gw_bf = sb("gw_bf", [128, 64], BF16)
        w0_bf = sb("w0_bf", [128, 8192], BF16)
        w1_bf = sb("w1_bf", [128, 8192], BF16)
        ogbc = sb("ogbc", [128, 1024], F32)
        obbc = sb("obbc", [128, 1024], F32)
        b0sb = sb("b0sb", [1, 1024], F32)
        b1sb = sb("b1sb", [1, 1024], F32)
        wbuf = sb("wbuf", [128, 1024], F32)
        o1r = sb("o1r", [1, 128], F32)
        identf = sb("identf", [128, 128], F32)
        identb = sb("identb", [128, 128], BF16)
        xt = sb("xt", [128, 1024], F32)
        x3 = sb("x3", [128, 1024], F32)
        obuf = sb("obuf", [128, 1024], F32)
        ebuf = sb("ebuf", [128, 1024], F32)
        x3T = sb("x3T", [128, 1024], BF16)
        otb = sb("otb", [128, 1024], BF16)
        pgt = sb("pgt", [128, 4096], BF16)
        pgT = sb("pgT", [128, 4096], BF16)
        wsm = sb("wsm", [128, 8], F32)
        gb8 = sb("gb8", [128, 8], F32)
        mxc = sb("mxc", [128, 1], F32)
        nmc = sb("nmc", [128, 1], F32)
        smc = sb("smc", [128, 1], F32)
        rsc = sb("rsc", [128, 1], F32)
        rmc = sb("rmc", [128, 1], F32)
        m0c = sb("m0c", [128, 1], F32)
        m1c = sb("m1c", [128, 1], F32)
        g0c = sb("g0c", [128, 1], F32)
        g1c = sb("g1c", [128, 1], F32)
        ccol = sb("ccol", [128, 1], F32)
        sumc = sb("sumc", [128, 1], F32)
        negm = sb("negm", [128, 1], F32)
        stdc = sb("stdc", [128, 1], F32)
        sqscr = sb("sqscr", [128, 512], F32)
        accc = sb("accc", [128, 4], F32)
        epsc = sb("epsc", [128, 1], F32)
        PA = st.enter_context(nc.psum_tensor("PA", [128, 512], F32))
        PB = st.enter_context(nc.psum_tensor("PB", [128, 512], F32))
        Ptf = st.enter_context(nc.psum_tensor("Ptf", [128, 128], F32))
        Ptb = st.enter_context(nc.psum_tensor("Ptb", [128, 128], BF16))
        Pg = st.enter_context(nc.psum_tensor("Pg", [128, 8], F32))
        S = st.enter_context(nc.semaphore("S"))
        block = st.enter_context(nc.Block())
        p = Prog()
        p.v(lambda e: e.memset(epsc[:], 1e-5))
        p.dma(identf[:], idf[:, :])
        p.dma(identb[:], idb[:, :])
        p.dma(o1r[:], onesr[:, :])
        p.dma(b0sb[:], b0r[:, :])
        p.dma(b1sb[:], b1r[:, :])
        for k in range(40):
            p.dma(wbuf[:], woT[128 * k:128 * (k + 1), :])
            p.v(lambda e, k=k: e.tensor_copy(out=wo_bf[:, 1024 * k:1024 * (k + 1)], in_=wbuf[:]))
        for k in range(8):
            p.dma(wbuf[:, 0:8], gw[128 * k:128 * (k + 1), :])
            p.v(lambda e, k=k: e.tensor_copy(out=gw_bf[:, 8 * k:8 * (k + 1)], in_=wbuf[:, 0:8]))
        for src, dst in ((w0T, w0_bf), (w1T, w1_bf)):
            for k in range(8):
                p.dma(wbuf[:], src[128 * k:128 * (k + 1), :])
                p.v(lambda e, dst=dst, k=k: e.tensor_copy(
                    out=dst[:, 1024 * k:1024 * (k + 1)], in_=wbuf[:]))
        for src, dst in ((og, ogbc), (ob, obbc)):
            p.dma(wbuf[0:1, :], src[:, :])
            for c in range(2):
                p.t(lambda e, c=c: e.matmul(out=PA[:], lhsT=o1r[0:1, :],
                                            rhs=wbuf[0:1, 512 * c:512 * (c + 1)],
                                            start=True, stop=True, skip_group_check=True))
                p.v(lambda e, dst=dst, c=c: e.tensor_copy(out=dst[:, 512 * c:512 * (c + 1)], in_=PA[:]))
        for tau in range(16):
            r0 = 128 * tau
            for k in range(8):
                p.dma(otb[:, 128 * k:128 * (k + 1)], oTs[128 * k:128 * (k + 1), r0:r0 + 128])
            p.dma(pgt[:], pgs[r0:r0 + 128, :])
            for m in range(32):
                p.t(lambda e, m=m: e.transpose(out=Ptb[:], in_=pgt[:, 128 * m:128 * (m + 1)],
                                               identity=identb[:]))
                p.v(lambda e, m=m: e.tensor_copy(out=pgT[:, 128 * m:128 * (m + 1)], in_=Ptb[:]))
            for E in range(2):
                PX = PA if E == 0 else PB
                for k in range(8):
                    p.t(lambda e, PX=PX, k=k, E=E: e.matmul(
                        out=PX[:], lhsT=otb[:, 128 * k:128 * (k + 1)],
                        rhs=wo_bf[:, 1024 * k + 512 * E:1024 * k + 512 * E + 512],
                        start=(k == 0), stop=False, skip_group_check=True))
                for m in range(32):
                    p.t(lambda e, PX=PX, m=m, E=E: e.matmul(
                        out=PX[:], lhsT=pgT[:, 128 * m:128 * (m + 1)],
                        rhs=wo_bf[:, 1024 * (8 + m) + 512 * E:1024 * (8 + m) + 512 * E + 512],
                        start=False, stop=(m == 31), skip_group_check=True))
                p.v(lambda e, PX=PX, E=E: e.tensor_copy(out=obuf[:, 512 * E:512 * E + 512], in_=PX[:]))
            ln_plain(p, obuf[:], 1024, sumc, negm, stdc, sqscr, accc, epsc)
            p.v(lambda e: e.tensor_tensor(out=obuf[:], in0=obuf[:], in1=ogbc[:], op=OP.mult))
            p.v(lambda e: e.tensor_tensor(out=obuf[:], in0=obuf[:], in1=obbc[:], op=OP.add))
            p.dma(xt[:], x2s[r0:r0 + 128, :])
            p.v(lambda e: e.tensor_tensor(out=x3[:], in0=xt[:], in1=obuf[:], op=OP.add))
            ln_plain(p, x3[:], 1024, sumc, negm, stdc, sqscr, accc, epsc)
            for k in range(8):
                p.t(lambda e, k=k: e.transpose(out=Ptf[:], in_=x3[:, 128 * k:128 * (k + 1)],
                                               identity=identf[:]))
                p.v(lambda e, k=k: e.tensor_copy(out=x3T[:, 128 * k:128 * (k + 1)], in_=Ptf[:]))
            for k in range(8):
                p.t(lambda e, k=k: e.matmul(out=Pg[:], lhsT=x3T[:, 128 * k:128 * (k + 1)],
                                            rhs=gw_bf[:, 8 * k:8 * (k + 1)],
                                            start=(k == 0), stop=(k == 7), skip_group_check=True))
            p.v(lambda e: e.reduce_max(out=mxc[:], in_=Pg[:], axis=AX))
            p.v(lambda e: e.tensor_scalar_mul(out=nmc[:], in0=mxc[:], scalar1=-1.0))
            p.s(lambda e: e.activation(out=wsm[:], in_=Pg[:], func=AF.Exp, bias=nmc[:],
                                       accum_out=smc[:]))
            p.v(lambda e: e.reciprocal(out=rsc[:], in_=smc[:]))
            p.v(lambda e: e.tensor_scalar_mul(out=wsm[:], in0=wsm[:], scalar1=rsc[:]))
            p.v(lambda e: e.reduce_max(out=rmc[:], in_=wsm[:, 1:8], axis=AX))
            p.v(lambda e: e.tensor_tensor(out=m0c[:], in0=wsm[:, 0:1], in1=rmc[:], op=OP.is_gt))
            p.v(lambda e: e.tensor_scalar(out=gb8[:], in0=wsm[:], scalar1=wsm[:, 1:2],
                                          scalar2=None, op0=OP.is_gt))
            p.v(lambda e: e.reduce_sum(out=ccol[:], in_=gb8[:], axis=AX))
            p.v(lambda e: e.tensor_scalar(out=m1c[:], in0=ccol[:], scalar1=1.0,
                                          scalar2=None, op0=OP.is_equal))
            p.v(lambda e: e.tensor_tensor(out=g0c[:], in0=wsm[:, 0:1], in1=m0c[:], op=OP.mult))
            p.v(lambda e: e.tensor_tensor(out=g1c[:], in0=wsm[:, 1:2], in1=m1c[:], op=OP.mult))
            for ei, (wbf, bsb, gcolv) in enumerate(((w0_bf, b0sb, g0c), (w1_bf, b1sb, g1c))):
                for E in range(2):
                    PX = PA if E == 0 else PB
                    for k in range(8):
                        p.t(lambda e, PX=PX, wbf=wbf, k=k, E=E: e.matmul(
                            out=PX[:], lhsT=x3T[:, 128 * k:128 * (k + 1)],
                            rhs=wbf[:, 1024 * k + 512 * E:1024 * k + 512 * E + 512],
                            start=(k == 0), stop=False, skip_group_check=True))
                    p.t(lambda e, PX=PX, bsb=bsb, E=E: e.matmul(
                        out=PX[:], lhsT=o1r[0:1, :], rhs=bsb[0:1, 512 * E:512 * E + 512],
                        start=False, stop=True, skip_group_check=True))
                    p.s(lambda e, PX=PX, E=E: e.activation(
                        out=ebuf[:, 512 * E:512 * E + 512], in_=PX[:], func=AF.Silu))
                if ei == 0:
                    p.v(lambda e, gcolv=gcolv: e.tensor_scalar_mul(out=obuf[:], in0=ebuf[:],
                                                                   scalar1=gcolv[:]))
                else:
                    p.v(lambda e, gcolv=gcolv: e.scalar_tensor_tensor(
                        out=obuf[:], in0=ebuf[:], scalar=gcolv[:], in1=obuf[:],
                        op0=OP.mult, op1=OP.add))
            p.v(lambda e: e.tensor_tensor(out=x3[:], in0=x3[:], in1=obuf[:], op=OP.add))
            ln_plain(p, x3[:], 1024, sumc, negm, stdc, sqscr, accc, epsc)
            p.dma(yo[r0:r0 + 128, :], x3[:])
        p.emit(nc, S, block)


def kernel(L0_attn_x, L0_attn_a, L0_attn_b, L0_attn_w_out, L0_attn_b_out,
           L0_ffn_fast_b, L0_ffn_w_fc, L0_ffn_w_proj, L0_ffn_u, L0_ffn_v, L0_ffn_gate,
           L1_attn_in_ln_g, L1_attn_in_ln_b, L1_attn_W_in, L1_attn_mid_ln_g,
           L1_attn_mid_ln_b, L1_attn_slopes, L1_attn_W_out, L1_attn_out_ln_g,
           L1_attn_out_ln_b, L1_ffn_gate_w, L1_ffn_w0, L1_ffn_b0, L1_ffn_w1, L1_ffn_b1):
    x = f32c(L0_attn_x)
    ident = np.eye(128, dtype=np.float32)
    identb = np.eye(128, dtype=np.float32).astype(BF)
    ones_r = np.ones((1, 128), dtype=np.float32)
    ones_c = np.ones((128, 1), dtype=np.float32)

    maps1 = []
    for c in range(NCORE):
        ck = slice(128 * c, 128 * (c + 1))
        maps1.append({
            "xs": f32c(x[:, :, ck]),
            "a_s": f32c(L0_attn_a[ck]).reshape(128, 1),
            "b_s": f32c(L0_attn_b[ck]).reshape(128, 1),
            "idf": ident,
        })
    r1 = run(build_scan, maps1)
    yT_full = np.stack([np.concatenate([r1[c]["yT"][b] for c in range(NCORE)], axis=0)
                        for b in range(B)], axis=0)

    shared2 = {
        "w_outT": f32c(L0_attn_w_out.T),
        "b_row": f32c(L0_attn_b_out).reshape(1, D),
        "w_fcT": f32c(L0_ffn_w_fc.T),
        "w_projT": f32c(L0_ffn_w_proj.T),
        "uT": f32c(L0_ffn_u.T),
        "vT": f32c(L0_ffn_v.T),
        "fbT": f32c(L0_ffn_fast_b.T),
        "gate": f32c(L0_ffn_gate).reshape(1, 1),
        "onesr": ones_r, "onesc": ones_c, "idf": ident,
    }
    maps2 = []
    for c in range(NCORE):
        b = c // 4
        t0 = 2048 * (c % 4)
        m = dict(shared2)
        m["xs"] = f32c(x[b, t0:t0 + TPC])
        m["yTs"] = f32c(yT_full[b][:, t0:t0 + TPC])
        maps2.append(m)
    r2 = run(build_l2, maps2)
    x2 = np.concatenate([r2[c]["x2o"] for c in range(NCORE)], axis=0).reshape(B, T, D)

    shared3 = {
        "g_in": f32c(L1_attn_in_ln_g).reshape(D, 1),
        "b_in": f32c(L1_attn_in_ln_b).reshape(D, 1),
        "winT": f32c(L1_attn_W_in.T),
        "mg": f32c(L1_attn_mid_ln_g).reshape(1, 7168),
        "mb": f32c(L1_attn_mid_ln_b).reshape(1, 7168),
        "onesr": ones_r, "idf": ident,
    }
    maps3 = []
    for c in range(NCORE):
        b = c // 4
        t0 = 2048 * (c % 4)
        m = dict(shared3)
        if t0 == 0:
            xh = np.concatenate([x2[b, 0:128], x2[b, 0:TPC]], axis=0)
            hcol = np.zeros((128, 1), dtype=np.float32)
        else:
            xh = x2[b, t0 - 128:t0 + TPC]
            hcol = np.ones((128, 1), dtype=np.float32)
        m["x2h"] = f32c(xh)
        m["halo"] = hcol
        maps3.append(m)
    r3 = run(build_l3, maps3)
    qkv = np.concatenate([r3[c]["qkvo"] for c in range(NCORE)], axis=0).reshape(B, T, 3072)
    pg = np.concatenate([r3[c]["pgo"] for c in range(NCORE)], axis=0).reshape(B, T, 4096)

    r_idx = np.arange(128)
    c_idx = np.arange(512)
    caus = np.concatenate(
        [np.where(r_idx[:, None] > c_idx[None, :] - 128 * v, np.float32(-1e10), np.float32(0.0))
         for v in range(4)], axis=1).astype(np.float32)
    iota2d = (r_idx[:, None] + 128 * np.arange(64)[None, :]).astype(np.float32)
    slopes = f32c(L1_attn_slopes)
    maps4 = []
    for c in range(NCORE):
        prs = [(pp // 16, pp % 16) for pp in range(4 * c, 4 * c + 4)]
        maps4.append({
            "q_in": bfc(np.stack([qkv[b, :, 64 * h:64 * h + 64] for (b, h) in prs])),
            "k_in": bfc(np.stack([qkv[b, :, 1024 + 64 * h:1024 + 64 * h + 64] for (b, h) in prs])),
            "v_in": bfc(np.stack([qkv[b, :, 2048 + 64 * h:2048 + 64 * h + 64] for (b, h) in prs])),
            "sl_in": np.stack([slopes[h:h + 1] for (b, h) in prs]),
            "iota": iota2d, "caus": caus, "idb": identb, "onesr": ones_r,
        })
    r4 = run(build_l4, maps4)
    oT_full = np.zeros((B, D, T), dtype=BF)
    for c in range(NCORE):
        for i, pp in enumerate(range(4 * c, 4 * c + 4)):
            b, h = pp // 16, pp % 16
            oT_full[b, 64 * h:64 * h + 64, :] = r4[c]["oT"][i]

    shared5 = {
        "woT": f32c(L1_attn_W_out.T),
        "og": f32c(L1_attn_out_ln_g).reshape(1, D),
        "ob": f32c(L1_attn_out_ln_b).reshape(1, D),
        "gw": f32c(L1_ffn_gate_w),
        "w0T": f32c(L1_ffn_w0.T),
        "b0r": f32c(L1_ffn_b0).reshape(1, D),
        "w1T": f32c(L1_ffn_w1.T),
        "b1r": f32c(L1_ffn_b1).reshape(1, D),
        "onesr": ones_r, "idf": ident, "idb": identb,
    }
    maps5 = []
    for c in range(NCORE):
        b = c // 4
        t0 = 2048 * (c % 4)
        m = dict(shared5)
        m["oTs"] = bfc(oT_full[b][:, t0:t0 + TPC])
        m["pgs"] = bfc(pg[b, t0:t0 + TPC])
        m["x2s"] = f32c(x2[b, t0:t0 + TPC])
        maps5.append(m)
    r5 = run(build_l5, maps5)
    out = np.concatenate([r5[c]["yo"] for c in range(NCORE)], axis=0).reshape(B, T, D)
    return out.astype(np.float32)
